# version 05
# baseline (speedup 1.0000x reference)
import numpy as np
from contextlib import ExitStack

import concourse.bass as bass
import concourse.mybir as mybir
from concourse.bass_utils import run_bass_kernel_spmd
from concourse.alu_op_type import AluOpType as OP

F32 = mybir.dt.float32
BF16 = mybir.dt.bfloat16
AF = mybir.ActivationFunctionType
AX = mybir.AxisListType

D = 2048
H = 16
E = 128
CH = 64
TOK = 2048
SUB = 512
NSUB = TOK // SUB
NTB = SUB // 128
NCH = SUB // CH
INTER = 1024
NEXP = 9
EPS_LN = 1e-5
EPS_RMS = 4e-6


class Tok:
    def __init__(self, ctx, nc, name):
        self.h = ctx.enter_context(nc.semaphore(name))
        self.n = 0

    def inc(self, instr, k=1):
        instr.then_inc(self.h, k)
        self.n += k
        return self.n


class DStream(Tok):

    def pre(self, eng):
        eng.wait_ge(self.h, self.n)

    def dma(self, instr):
        instr.then_inc(self.h, 16)
        self.n += 16
        return self.n


class RecEngine:

    def __init__(self, tagf=None):
        self.ops = []
        self.tagf = tagf or (lambda: "")

    def __getattr__(self, name):
        if name.startswith("_"):
            raise AttributeError(name)
        ops = self.ops
        tagf = self.tagf

        def call(*a, **k):
            entry = (name, a, k, [], tagf())
            ops.append(entry)

            class _I:
                def then_inc(self, sem, val):
                    entry[3].append((sem, val))
                    return self
            return _I()
        return call

    def replay(self, eng):
        for name, a, k, incs, _tag in self.ops:
            ins = getattr(eng, name)(*a, **k)
            for sem, val in incs:
                ins.then_inc(sem, val)


class Builder:
    def __init__(self, dbg=False):
        self.dbg = dbg
        nc = self.nc = bass.Bass("TRN2", target_bir_lowering=False)
        ap = lambda t: t.ap()
        self.x_own = ap(nc.dram_tensor("x_own", [TOK, D], F32, kind="ExternalInput"))
        self.x_pre = ap(nc.dram_tensor("x_pre", [TOK, D], F32, kind="ExternalInput"))
        self.pre_sc = ap(nc.dram_tensor("pre_sc", [128, 1], F32, kind="ExternalInput"))
        self.wq_t = ap(nc.dram_tensor("wq_t", [H, 128, D], F32, kind="ExternalInput"))
        self.wk_t = ap(nc.dram_tensor("wk_t", [H, 128, D], F32, kind="ExternalInput"))
        self.wv_t = ap(nc.dram_tensor("wv_t", [8, 128, 16 * 256], F32, kind="ExternalInput"))
        self.wo_t = ap(nc.dram_tensor("wo_t", [4, 2, 128, 8 * 512], F32, kind="ExternalInput"))
        self.nw_t = ap(nc.dram_tensor("nw_t", [128, 16], F32, kind="ExternalInput"))
        self.gw_t = ap(nc.dram_tensor("gw_t", [128, 16 * 8], F32, kind="ExternalInput"))
        self.w1_t = ap(nc.dram_tensor("w1_t", [NEXP, 8, 128, 16 * 128], F32, kind="ExternalInput"))
        self.w3_t = ap(nc.dram_tensor("w3_t", [NEXP, 8, 128, 16 * 128], F32, kind="ExternalInput"))
        self.w2_t = ap(nc.dram_tensor("w2_t", [NEXP, 8, 128, D], F32, kind="ExternalInput"))
        self.out = ap(nc.dram_tensor("out", [TOK, D], F32, kind="ExternalOutput"))
        self.w1c = ap(nc.dram_tensor("w1c", [NEXP, 8, 128, 16 * 128], BF16, kind="Internal"))
        self.w3c = ap(nc.dram_tensor("w3c", [NEXP, 8, 128, 16 * 128], BF16, kind="Internal"))
        self.w2c = ap(nc.dram_tensor("w2c", [NEXP, 8, 128, D], BF16, kind="Internal"))

        self.ctx = ExitStack()
        en = self.ctx.enter_context
        sb = lambda name, shp, dt: en(nc.sbuf_tensor(name, shp, dt))
        self.y1g = sb("y1g", [128, 4 * D], BF16)
        self.xln = sb("xln", [128, 2 * 4 * D], BF16)
        self.yac = sb("yac", [128, 4 * D], F32)
        self.fm0 = sb("fm0", [128, 16 * SUB], BF16)
        self.vcb = sb("vcb", [128, 4 * D], BF16)
        self.ob = sb("ob", [128, H * SUB], BF16)
        self.wA = sb("wA", [128, 2 * D], BF16)
        self.wB = sb("wB", [128, 2 * D], BF16)
        self.wC = sb("wC", [128, 2 * 4096], BF16)
        self.th2 = sb("th2", [128, 2 * SUB], F32)
        self.usb = sb("usb", [128, 2 * SUB], F32)
        self.f2s = sb("f2s", [128, 2 * SUB], F32)
        self.Ep = sb("Ep", [128, 2 * SUB], F32)
        self.Em = sb("Em", [128, 2 * SUB], BF16)
        self.th1 = sb("th1", [128, 2 * SUB], BF16)
        self.qh = sb("qh", [128, 4 * SUB], BF16)
        self.kdec = self.Em
        self.krem = sb("krem", [128, 4 * SUB], BF16)
        self.dec4 = sb("dec4", [128, 4 * NCH], F32)
        self.krtm = sb("krtm", [128, 2 * 128], BF16)
        self.scb = sb("scb", [128, 2 * 256], BF16)
        self.Ssb = sb("Ssb", [128, H * E], F32)
        self.Sb16 = sb("Sb16", [128, H * E], BF16)
        self.sqb = sb("sqb", [128, SUB], BF16)
        self.umask = sb("umask", [64, SUB], BF16)
        self.iden = sb("iden", [128, 128], BF16)
        self.id1 = sb("id1", [1, 1], BF16)
        self.onec = sb("onec", [128, 1], BF16)
        self.onech = sb("onech", [128, CH], F32)
        self.nwsb = sb("nwsb", [128, 16], F32)
        self.gwsb = sb("gwsb", [128, 16 * 8], BF16)
        self.psc = sb("psc", [128, 1], F32)
        self.bn24 = sb("bn24", [128, 24], F32)
        self.lnst = sb("lnst", [128, 16], F32)
        self.rr = sb("rr", [1, SUB], BF16)
        self.rcol = sb("rcol", [128, 4], F32)
        self.comb = sb("comb", [128, 4 * 8], F32)
        self.smw = sb("smw", [128, 16], F32)
        self.cst = sb("cst", [128, 8], F32)
        self.ps = en(nc.psum_tensor("ps", [128, 7, 512], F32))
        self.psb = en(nc.psum_tensor("psb", [128, 1024], BF16))
        self.XTO = 3 * D

        self.CONSTS = [0.5, 0.25, -0.5, EPS_LN, EPS_RMS, 1.0 / D, 1.0, 256.0]
        for i, v in enumerate(self.CONSTS):
            nc.const_aps.register(F32, float(v), self.cst[:, i:i + 1])

        c = self.ctx
        self.sT = Tok(c, nc, "sT")
        self.sV = Tok(c, nc, "sV")
        self.sS = Tok(c, nc, "sS")
        self.sG = Tok(c, nc, "sG")
        self.xS = [DStream(c, nc, "xS0"), DStream(c, nc, "xS1")]
        self.wkS = [DStream(c, nc, "wkS0"), DStream(c, nc, "wkS1")]
        self.wqS = [DStream(c, nc, "wqS0"), DStream(c, nc, "wqS1")]
        self.wvS = [DStream(c, nc, "wvS0"), DStream(c, nc, "wvS1")]
        self.woS = [DStream(c, nc, "woS0"), DStream(c, nc, "woS1")]
        self.w13S = [DStream(c, nc, "w13S0"), DStream(c, nc, "w13S1")]
        self.w2S = DStream(c, nc, "w2S")
        self.outS = DStream(c, nc, "outS")
        self.iniS = DStream(c, nc, "iniS")
        self.wbS = DStream(c, nc, "wbS")
        self.w2Sh = DStream(c, nc, "w2Sh")
        self.w13Sh = [DStream(c, nc, "w13Sh0"), DStream(c, nc, "w13Sh1")]
        self.ffn_pass = 0

        self.xtok = {}
        self.subcnt = 0
        self.wv_used = [0, 0]
        self.wk_used = [0, 0]
        self.wq_used = [0, 0]
        self.wo_used = [0, 0]
        self.w13_used = [0, 0]
        self.w2_used = 0
        self.projbank_free = [0, 0]
        self.projbank_freeS = [0, 0]
        self.sc_free = 0
        self.ob_free = [0, 0]
        self.sq_used = 0
        self.p_stt = {}
        self.p_sb16 = {}
        self.slot_state_free = [0, 0]
        self.chain_free = [0, 0]
        self.scb_used = [0, 0]
        self.yac_free = 0
        self._silu_tok = 0
        self._gp_tok = 0
        self._g_done = 0
        self._g_done_prev = 0
        self._yst_tok = [0, 0]
        self._ycp_tok = [0, 0, 0, 0]
        self._w2_ldtok = {}
        self.w2_tok = 0
        self._wb_tok = [0, 0]
        self._wb2_tok = 0
        self.y1_tok = 0

    def T(self, f): f(self.rec["tensor"])
    def V(self, f): f(self.rec["vector"])
    def S(self, f): f(self.rec["scalar"])
    def G(self, f): f(self.rec["gpsimd"])
    def SY(self, f): f(self.rec["sync"])

    def build(self):
        nc = self.nc
        self.tag = "init"
        self.rec = {k: RecEngine(lambda: self.tag) for k in ("tensor", "vector", "scalar", "gpsimd", "sync")}
        self._init()
        for sidx in range(NSUB):
            self._attn_subtile(sidx, own=False)
        self._sb16_refresh()
        for sidx in range(NSUB):
            self._attn_subtile(sidx, own=True)
            self._ffn(sidx)
        s = self
        self.SY(lambda sy: sy.wait_ge(s.outS.h, s.outS.n))
        with self.ctx:
            with nc.Block() as blk:
                blk.tensor(self.rec["tensor"].replay)
                blk.vector(self.rec["vector"].replay)
                blk.scalar(self.rec["scalar"].replay)
                blk.gpsimd(self.rec["gpsimd"].replay)
                blk.sync(self.rec["sync"].replay)
        return nc

    def _init(self):
        s = self

        def vinit(v):
            for i, val in enumerate(s.CONSTS):
                v.memset(s.cst[:, i:i + 1], float(val))
            v.memset(s.onec[:, :], 1.0)
            v.memset(s.onech[:, :], 1.0)
            v.memset(s.Ssb[:, :], 0.0)
            s.sV.inc(v.memset(s.Sb16[:, :], 0.0))
        self.V(vinit)

        def ginit(g):
            g.iota(s.f2s[0:64, 0:SUB], [[0, NCH], [1, CH]], channel_multiplier=-1,
                   allow_small_or_imprecise_dtypes=True)
            g.iota(s.f2s[:, 512:640], [[1, 128]], channel_multiplier=-1,
                   allow_small_or_imprecise_dtypes=True)
            s.sG.inc(g.memset(s.id1[:, :], 1.0))
            s.iniS.pre(g)
            s.iniS.dma(g.dma_start(s.nwsb[:, :], s.nw_t[:, :]))
            s.iniS.pre(g)
            s.iniS.dma(g.dma_start(s.psc[:, :], s.pre_sc[:, :]))
            s.iniS.pre(g)
            s.iniS.dma(g.dma_start(s.gwsb[:, :], s.gw_t[:, :]))
        self.G(ginit)
        gtok = s.sG.n

        def vinit2(v):
            v.wait_ge(s.sG.h, gtok)
            v.wait_ge(s.sV.h, s.sV.n)
            v.tensor_single_scalar(s.umask[:, :], s.f2s[0:64, 0:SUB], 0.0, OP.is_ge)
            s.sV.inc(v.tensor_single_scalar(s.f2s[:, 512:640], s.f2s[:, 512:640], 0.0,
                                            OP.is_equal))
            v.wait_ge(s.sV.h, s.sV.n)
            s.sV.inc(v.tensor_copy(s.iden[:, :], s.f2s[:, 512:640]))
        self.V(vinit2)

        def xload(gp):
            for u in (0, 1):
                s.xS[u % 2].pre(gp)
                s.xS[u % 2].dma(gp.dma_start(
                    s.xln[:, (u % 2) * 4 * D:(u % 2 + 1) * 4 * D].rearrange("p (a d) -> p a d", d=D),
                    s.x_pre[u * SUB:(u + 1) * SUB, :].rearrange("(a p) d -> p a d", p=128)))
                s.xtok[u] = s.xS[u % 2].n
        self.G(xload)

    def _sb16_refresh(self):
        s = self

        def cp(v):
            v.wait_ge(s.sV.h, s.sV.n)
            s.sV.inc(v.tensor_copy(s.Sb16[:, :], s.Ssb[:, :]))
        self.V(cp)
        for h in range(H):
            s.p_sb16[h] = s.sV.n

    def _prefetch_x(self, after_v_tok):
        s = self
        u = self.subcnt + 2
        if u >= 2 * NSUB:
            return
        own2 = u >= NSUB
        src = s.x_own if own2 else s.x_pre
        r0 = (u % NSUB) * SUB
        buf = u % 2

        def ld(gp):
            gp.wait_ge(s.sV.h, after_v_tok)
            s.xS[buf].pre(gp)
            s.xS[buf].dma(gp.dma_start(
                s.xln[:, buf * 4 * D:(buf + 1) * 4 * D].rearrange("p (a d) -> p a d", d=D),
                src[r0:r0 + SUB, :].rearrange("(a p) d -> p a d", p=128)))
            s.xtok[u] = s.xS[buf].n
        self.G(ld)

    def _ln_into_fm0(self, src_y1, tok_src):
        self.tag = "ln"
        """LN (no affine) of 4 tblocks -> bf16 feature-major in fm0.
        src: xln[cur buf] (attn, tok_src = dX token) or y1g (ffn, tok_src = sV).
        Returns (S token: fm0 ready, V token: src fully read)."""
        s = self
        buf = self.subcnt % 2
        last_v = 0
        for tb in range(NTB):
            x_tb = (s.y1g[:, tb * D:(tb + 1) * D] if src_y1 else
                    s.xln[:, buf * 4 * D + tb * D:buf * 4 * D + (tb + 1) * D])

            def stats(v, x_tb=x_tb, tb=tb):
                if src_y1:
                    v.wait_ge(s.sV.h, tok_src)
                else:
                    v.wait_ge(s.xS[self.subcnt % 2].h, tok_src)
                v.wait_ge(s.sV.h, s.sV.n)
                for j in range(4):
                    ins = v.bn_stats(s.bn24[:, 6 * j:6 * (j + 1)], x_tb[:, 512 * j:512 * (j + 1)])
                s.sV.inc(ins)
                ms = s.lnst[:, 4 * tb:4 * tb + 1]
                vr = s.lnst[:, 4 * tb + 1:4 * tb + 2]
                t2 = s.lnst[:, 4 * tb + 2:4 * tb + 3]
                v.wait_ge(s.sV.h, s.sV.n)
                v.tensor_tensor(s.smw[:, 0:8], s.bn24[:, 1:24:3], s.bn24[:, 1:24:3], OP.mult)
                v.reduce_sum(ms, s.bn24[:, 1:24:3], axis=AX.X)
                s.sV.inc(v.reduce_sum(vr, s.bn24[:, 2:24:3], axis=AX.X))
                v.wait_ge(s.sV.h, s.sV.n)
                v.reduce_sum(t2, s.smw[:, 0:8], axis=AX.X)
                s.sV.inc(v.tensor_scalar(ms, ms, 0.125, None, OP.mult))
                v.wait_ge(s.sV.h, s.sV.n)
                s.sV.inc(v.scalar_tensor_tensor(vr, t2, 256.0, vr, OP.mult, OP.add))
                v.wait_ge(s.sV.h, s.sV.n)
                v.tensor_scalar(vr, vr, 1.0 / D, None, OP.mult)
                s.sV.inc(v.tensor_tensor(t2, ms, ms, OP.mult))
                v.wait_ge(s.sV.h, s.sV.n)
                s.sV.inc(v.tensor_tensor(vr, vr, t2, OP.subtract))
            self.V(stats)
            tok_var = s.sV.n

            def rstd(sc, tb=tb, tok_var=tok_var):
                sc.wait_ge(s.sV.h, tok_var)
                sc.wait_ge(s.sS.h, s.sS.n)
                vr = s.lnst[:, 4 * tb + 1:4 * tb + 2]
                t2 = s.lnst[:, 4 * tb + 2:4 * tb + 3]
                rs = s.lnst[:, 4 * tb + 3:4 * tb + 4]
                s.sS.inc(sc.activation(t2, vr, AF.Ln, bias=EPS_LN))
                sc.wait_ge(s.sS.h, s.sS.n)
                s.sS.inc(sc.activation(rs, t2, AF.Exp, scale=-0.5))
            self.S(rstd)
            tok_rs = s.sS.n

            def norm(v, x_tb=x_tb, tb=tb, tok_rs=tok_rs):
                v.wait_ge(s.sS.h, tok_rs)
                v.wait_ge(s.sT.h, s.sT.n)
                v.wait_ge(s.sV.h, s.sV.n)
                ms = s.lnst[:, 4 * tb:4 * tb + 1]
                rs = s.lnst[:, 4 * tb + 3:4 * tb + 4]
                s.sV.inc(v.tensor_scalar(s.vcb[:, s.XTO:s.XTO + D], x_tb, ms, rs, OP.subtract, OP.mult))
            self.V(norm)
            tok_xn = s.sV.n
            last_v = tok_xn

            for g4 in range(4):
                def tr(t, g4=g4, tok_xn=tok_xn):
                    t.wait_ge(s.sV.h, tok_xn)
                    t.wait_ge(s.sS.h, s.sS.n)
                    for kk in range(4):
                        k = g4 * 4 + kk
                        ins = t.transpose(s.psb[:, kk * 128:(kk + 1) * 128],
                                          s.vcb[:, s.XTO + k * 128:s.XTO + (k + 1) * 128], s.iden[:, :])
                    s.sT.inc(ins)
                self.T(tr)
                tok_tr = s.sT.n

                def ev(sc, tb=tb, g4=g4, tok_tr=tok_tr):
                    sc.wait_ge(s.sT.h, tok_tr)
                    sc.wait_ge(s.sS.h, s.sS.n)
                    dst = s.fm0[:, :].rearrange("p (k t) -> p k t", t=SUB)[
                        :, 4 * g4:4 * g4 + 4, tb * 128:(tb + 1) * 128]
                    src = s.psb[:, 0:512].rearrange("p (k t) -> p k t", t=128)
                    s.sS.inc(sc.copy(dst, src))
                self.S(ev)
        return s.sS.n, last_v

    def _attn_subtile(self, sidx, own):
        s = self
        tok_x = s.xtok[self.subcnt]
        buf = self.subcnt % 2

        fm_tok, ln_v = self._ln_into_fm0(False, tok_x)
        vc_tok = self._vproj(fm_tok, own)
        if not own:
            self._prefetch_x(ln_v)
        self._pctx = {}
        self._run_pairs(own, fm_tok, vc_tok)
        if own:
            self._rms_wo(sidx, buf)
            self._prefetch_x(s.sV.n)
        self.subcnt += 1

    def _vproj(self, fm_tok, own):
        self.tag = "vproj"
        s = self
        for g in range(8):
            wbuf = g % 2

            def ld(gp, g=g, wbuf=wbuf):
                gp.wait_ge(s.sT.h, max(s.wv_used[wbuf], s.wo_used[wbuf]))
                gp.wait_ge(s.wbS.h, s._wb2_tok)
                s.wvS[wbuf].pre(gp)
                s.wvS[wbuf].dma(gp.dma_start(s.wC[:, wbuf * 4096:(wbuf + 1) * 4096],
                                             s.wv_t[g, :, :]))
            self.G(ld)
            tok_w = s.wvS[wbuf].n

            sS_now = s.sS.n

            def mm(t, g=g, wbuf=wbuf, tok_w=tok_w, fm_tok=fm_tok, sS_now=sS_now):
                t.wait_ge(s.wvS[wbuf].h, tok_w)
                t.wait_ge(s.sS.h, max(fm_tok, sS_now))
                t.wait_ge(s.sV.h, s.sV.n)
                for tb in range(NTB):
                    for k in range(16):
                        ins = t.matmul(s.ps[:, 2 + tb, 0:256],
                                       s.fm0[:, k * SUB + tb * 128:k * SUB + (tb + 1) * 128],
                                       s.wC[:, wbuf * 4096 + k * 256:wbuf * 4096 + (k + 1) * 256],
                                       start=(k == 0), stop=(k == 15))
                    s.sT.inc(ins)
            self.T(mm)
            s.wv_used[wbuf] = s.sT.n
            tok_mm = s.sT.n

            def ev(v, g=g, tok_mm=tok_mm, own=own):
                v.wait_ge(s.sT.h, tok_mm)
                v.wait_ge(s.sV.h, s.sV.n)
                for tb in range(NTB):
                    dst = s.vcb[:, tb * D + g * 256:tb * D + (g + 1) * 256]
                    if own:
                        ins = v.tensor_copy(dst, s.ps[:, 2 + tb, 0:256])
                    else:
                        if g == 0 and tb == 0:
                            v.wait_ge(s.iniS.h, 32)
                        ins = v.tensor_scalar(dst, s.ps[:, 2 + tb, 0:256],
                                              s.psc[:, 0:1], None, OP.mult)
                s.sV.inc(ins)
            self.V(ev)
        return s.sV.n

    def _run_pairs(self, own, fm_tok, vc_tok):
        s = self
        prev = None
        for hp in range(H // 2):
            ctx = {"sS0": s.sS.n, "hp": hp}
            fill = self._pair_front(own, hp, fm_tok, ctx)
            if prev is None:
                for f in fill:
                    f()
            else:
                self._pair_chunks(own, prev, fill, vc_tok)
                self._pair_post(own, prev)
            if own:
                self._scores_pair(hp, ctx)
            self._pctx[hp] = ctx
            prev = hp
        self._pair_chunks(own, prev, [], vc_tok)
        self._pair_post(own, prev)

    def _pair_front(self, own, hp, fm_tok, ctx):
        fill = []
        for i, h in enumerate((2 * hp, 2 * hp + 1)):
            fill += self._chain_steps(own, h, i, hp, fm_tok, ctx)
        return fill

    def _chain_steps(self, own, h, i, hp, fm_tok, ctx):
        self.tag = "chain"
        s = self
        wbuf = h % 2
        pb = h % 2
        so = i * SUB
        so4 = ((hp % 2) * 2 + i) * SUB
        sd4 = ((hp % 2) * 2 + i) * NCH
        steps = []

        def ldk():
            def f(gp, h=h, wbuf=wbuf, own=own):
                gp.wait_ge(s.sT.h, max(s.wk_used[wbuf], s.w13_used[wbuf]))
                gp.wait_ge(s.wbS.h, s._wb_tok[wbuf])
                s.wkS[wbuf].pre(gp)
                s.wkS[wbuf].dma(gp.dma_start(s.wB[:, wbuf * D:(wbuf + 1) * D],
                                             s.wk_t[h, :, :]))
                if own:
                    gp.wait_ge(s.sT.h, max(s.wq_used[wbuf], s.w13_used[wbuf]))
                    s.wqS[wbuf].pre(gp)
                    s.wqS[wbuf].dma(gp.dma_start(s.wA[:, wbuf * D:(wbuf + 1) * D],
                                                 s.wq_t[h, :, :]))
            self.G(f)
            ctx[("wk", i)] = s.wkS[wbuf].n
            ctx[("wq", i)] = s.wqS[wbuf].n
        steps.append(ldk)

        def fmm():
            tok_wk = ctx[("wk", i)]

            def f(t, pb=pb, wbuf=wbuf, tok_wk=tok_wk, fm_tok=fm_tok):
                t.wait_ge(s.wkS[wbuf].h, tok_wk)
                t.wait_ge(s.sS.h, max(fm_tok, s.projbank_freeS[pb]))
                t.wait_ge(s.sV.h, s.projbank_free[pb])
                for k in range(16):
                    ins = t.matmul(s.ps[:, pb, :],
                                   s.wB[:, wbuf * D + k * 128:wbuf * D + (k + 1) * 128],
                                   s.fm0[:, k * SUB:(k + 1) * SUB],
                                   start=(k == 0), stop=(k == 15))
                s.sT.inc(ins)
            self.T(f)
            s.wk_used[wbuf] = s.sT.n
            ctx[("fmm", i)] = s.sT.n
        steps.append(fmm)

        def th1e():
            tok_fmm = ctx[("fmm", i)]

            def f(sc, so=so, pb=pb, tok_fmm=tok_fmm):
                sc.wait_ge(s.sT.h, tok_fmm)
                sc.wait_ge(s.sS.h, s.sS.n)
                s.sS.inc(sc.activation(s.th1[:, so:so + SUB], s.ps[:, pb, :],
                                       AF.Tanh, scale=0.5))
            self.S(f)
            ctx[("th1", i)] = s.sS.n
        steps.append(th1e)

        def th2u():
            def f(sc, so=so):
                sc.wait_ge(s.sS.h, s.sS.n)
                sc.activation(s.th2[:, so:so + SUB], s.th1[:, so:so + SUB], AF.Tanh,
                              scale=0.25, bias=0.25)
                s.sS.inc(sc.activation(s.usb[:, so:so + SUB], s.th1[:, so:so + SUB],
                                       AF.Exp, scale=-0.5, bias=-0.5))
            self.S(f)
            ctx[("th2", i)] = s.sS.n
        steps.append(th2u)

        if own:
            def qmm():
                tok_wq = ctx[("wq", i)]
                tok_th1 = ctx[("th1", i)]

                def f(t, pb=pb, wbuf=wbuf, tok_wq=tok_wq, tok_th1=tok_th1):
                    t.wait_ge(s.wqS[wbuf].h, tok_wq)
                    t.wait_ge(s.sS.h, tok_th1)
                    for k in range(16):
                        ins = t.matmul(s.ps[:, pb, :],
                                       s.wA[:, wbuf * D + k * 128:wbuf * D + (k + 1) * 128],
                                       s.fm0[:, k * SUB:(k + 1) * SUB],
                                       start=(k == 0), stop=(k == 15))
                    s.sT.inc(ins)
                self.T(f)
                s.wq_used[wbuf] = s.sT.n
                ctx[("qmm", i)] = s.sT.n
            steps.append(qmm)

            def th1qe():
                tok_qmm = ctx[("qmm", i)]
                tok_th2 = ctx[("th2", i)]

                def f(sc, so=so, pb=pb, tok_qmm=tok_qmm, tok_th2=tok_th2):
                    sc.wait_ge(s.sT.h, tok_qmm)
                    sc.wait_ge(s.sS.h, max(tok_th2, s.sS.n))
                    s.sS.inc(sc.activation(s.th1[:, so:so + SUB], s.ps[:, pb, :],
                                           AF.Tanh, scale=0.5))
                self.S(f)
                ctx[("th1q", i)] = s.sS.n
            steps.append(th1qe)

        def chainv():
            tok_th2 = ctx[("th2", i)]

            def f(v, so=so, so4=so4, sd4=sd4, i=i, tok_th2=tok_th2):
                v.wait_ge(s.sS.h, tok_th2)
                v.wait_ge(s.sT.h, s.chain_free[i])
                v.wait_ge(s.sV.h, s.sV.n)
                v.tensor_scalar(s.f2s[:, so:so + SUB], s.th2[:, so:so + SUB], 0.5, 0.5,
                                OP.mult, OP.add)
                s.sV.inc(v.tensor_single_scalar(s.usb[:, so:so + SUB],
                                                s.usb[:, so:so + SUB], 1.0, OP.add))
                v.wait_ge(s.sV.h, s.sV.n)
                for c in range(NCH):
                    sl = slice(so + c * CH, so + (c + 1) * CH)
                    v.tensor_tensor_scan(s.Ep[:, sl], s.f2s[:, sl], s.onech[:, :],
                                         1.0, OP.mult, OP.mult)
                for c in range(NCH):
                    sl = slice(so + c * CH, so + (c + 1) * CH)
                    ins = v.tensor_tensor_scan(s.Em[:, sl], s.usb[:, sl], s.onech[:, :],
                                               0.5, OP.mult, OP.mult)
                s.sV.inc(ins)
                v.wait_ge(s.sV.h, s.sV.n)
                v.tensor_copy(s.dec4[:, sd4:sd4 + NCH],
                              s.Ep[:, so + CH - 1:so + SUB:CH])
                s.sV.inc(v.tensor_scalar(s.f2s[:, so:so + SUB], s.th2[:, so:so + SUB],
                                         -1.0, 1.0, OP.mult, OP.add))
                v.wait_ge(s.sV.h, s.sV.n)
                s.sV.inc(v.tensor_tensor(s.kdec[:, so:so + SUB], s.f2s[:, so:so + SUB],
                                         s.Em[:, so:so + SUB], OP.mult))
                v.wait_ge(s.sV.h, s.sV.n)
                for c in range(NCH):
                    ins = v.tensor_scalar(
                        s.krem[:, so4 + c * CH:so4 + (c + 1) * CH],
                        s.kdec[:, so + c * CH:so + (c + 1) * CH],
                        s.Ep[:, so + c * CH + CH - 1:so + c * CH + CH], None, OP.mult)
                s.sV.inc(ins)
            self.V(f)
            ctx[("chain", i)] = s.sV.n
            if not own:
                s.projbank_free[pb] = s.sV.n
        steps.append(chainv)

        if own:
            def qv():
                tok_th1q = ctx[("th1q", i)]

                def f(v, so=so, so4=so4, pb=pb, tok_th1q=tok_th1q):
                    v.wait_ge(s.sS.h, tok_th1q)
                    v.wait_ge(s.sV.h, s.sV.n)
                    s.sV.inc(v.scalar_tensor_tensor(s.usb[:, so:so + SUB],
                                                    s.th1[:, so:so + SUB], 1.0,
                                                    s.ps[:, pb, :], OP.add, OP.mult))
                    v.wait_ge(s.sV.h, s.sV.n)
                    s.sV.inc(v.tensor_tensor(s.qh[:, so4:so4 + SUB],
                                             s.usb[:, so:so + SUB],
                                             s.Ep[:, so:so + SUB], OP.mult))
                self.V(f)
                s.projbank_free[pb] = s.sV.n
                ctx[("qv", i)] = s.sV.n
            steps.append(qv)
        return steps

    def _scores_pair(self, hp, ctx):
        self.tag = "scores"
        s = self
        prev_ctx = self._pctx.get(hp - 1)
        for i in range(2):
            so4 = ((hp % 2) * 2 + i) * SUB
            tok_chain = ctx[("qv", i)]

            def mm(t, so4=so4, tok_chain=tok_chain):
                t.wait_ge(s.sV.h, max(tok_chain, s.sc_free))
                for c in range(NCH):
                    ins = t.matmul(s.ps[0:64, 3, c * CH:(c + 1) * CH],
                                   s.kdec[:, (so4 % (2 * SUB)) + c * CH:
                                          (so4 % (2 * SUB)) + (c + 1) * CH],
                                   s.qh[:, so4 + c * CH:so4 + (c + 1) * CH],
                                   start=True, stop=True)
                s.sT.inc(ins)
            self.T(mm)
            tok_sc = s.sT.n
            s.chain_free[i] = s.sT.n

            def mask(v, i=i, tok_sc=tok_sc, prev_ctx=prev_ctx):
                v.wait_ge(s.sT.h, tok_sc)
                if prev_ctx is not None:
                    v.wait_ge(s.sT.h, prev_ctx.get(("omm", i), 0))
                v.wait_ge(s.sV.h, s.sV.n)
                psv = s.ps[0:64, 3, :].rearrange("p (c w) -> p c w", w=CH)
                mk = s.umask[:, :].rearrange("p (c w) -> p c w", w=CH)[:, 0:4, :]
                v.tensor_tensor(s.scb[0:64, i * 256:(i + 1) * 256].rearrange(
                    "p (c w) -> p c w", w=CH), psv[:, 0::2, :], mk, OP.mult)
                s.sV.inc(v.tensor_tensor(s.scb[64:128, i * 256:(i + 1) * 256].rearrange(
                    "p (c w) -> p c w", w=CH), psv[:, 1::2, :], mk, OP.mult))
            self.V(mask)
            s.sc_free = s.sV.n
            ctx[("sc", i)] = s.sV.n

    def _pair_chunks(self, own, hp, fill, vc_tok):
        s = self
        ctx = self._pctx[hp]
        fill = list(fill)
        nf = len(fill)
        for c in range(NCH):
            for i, h in enumerate((2 * hp, 2 * hp + 1)):
                self._chunk(own, h, i, hp, c, ctx, vc_tok)
            take = (nf * (c + 1)) // NCH - (nf * c) // NCH
            for _ in range(take):
                fill.pop(0)()
        for f in fill:
            f()

    def _pair_post(self, own, hp):
        s = self
        ctx = self._pctx[hp]
        for i, h in enumerate((2 * hp, 2 * hp + 1)):
            s.p_stt[h] = ctx[("stt", i)]
            s.slot_state_free[i] = ctx[("stt", i)]
            if own:
                s.p_sb16[h] = ctx[("sb16", i)]
        if own:
            for i, h in enumerate((2 * hp, 2 * hp + 1)):
                self._o_evac(h, i, ctx, first=(hp == 0 and i == 0),
                             last=(hp == H // 2 - 1 and i == 1))

    def _chunk(self, own, h, i, hp, c, ctx, vc_tok):
        self.tag = "chunk"
        s = self
        so4 = ((hp % 2) * 2 + i) * SUB
        sd4 = ((hp % 2) * 2 + i) * NCH
        stb = 2 if i == 0 else 3
        obk = 4 + i
        tbl = (c * CH) // 128
        pbase = (c * CH) % 128
        vsl = s.vcb[pbase:pbase + CH, tbl * D + h * E:tbl * D + (h + 1) * E]

        if c % 2 == 0:
            def tr(t, so4=so4, i=i, c=c, stb=stb):
                t.wait_ge(s.sV.h, ctx.get(("stt", i), s.slot_state_free[i]))
                t.wait_ge(s.sV.h, ctx[("chain", i)])
                t.wait_ge(s.sS.h, max(ctx.get(("krtm", i), 0), ctx["sS0"]))
                s.sT.inc(t.transpose(s.psb[:, 512 + i * 128:512 + (i + 1) * 128],
                                     s.krem[:, so4 + c * CH:so4 + (c + 2) * CH],
                                     s.iden[:, :]))
            self.T(tr)
            tok_tr = s.sT.n

            def ev(sc, i=i, tok_tr=tok_tr):
                sc.wait_ge(s.sT.h, max(tok_tr, ctx.get(("ds", i), 0)))
                sc.wait_ge(s.sS.h, s.sS.n)
                s.sS.inc(sc.copy(s.krtm[:, i * 128:(i + 1) * 128],
                                 s.psb[:, 512 + i * 128:512 + (i + 1) * 128]))
            self.S(ev)
            ctx[("krtm", i)] = s.sS.n

        if own:
            def omm(t, so4=so4, i=i, c=c, h=h, obk=obk, vsl=vsl, pbase=pbase):
                t.wait_ge(s.sV.h, ctx[("sc", i)])
                t.wait_ge(s.sV.h, vc_tok)
                t.wait_ge(s.sV.h, ctx.get(("sb16", i), s.p_sb16.get(h, 0)))
                if c == 0:
                    t.wait_ge(s.sS.h, s.ob_free[i])
                sl = slice(c * CH, (c + 1) * CH)
                scsl = s.scb[pbase:pbase + CH,
                             i * 256 + (c // 2) * CH:i * 256 + (c // 2 + 1) * CH]
                t.matmul(s.ps[:, obk, sl], vsl, scsl, start=True, stop=False)
                ins = t.matmul(s.ps[:, obk, sl], s.Sb16[:, h * E:(h + 1) * E],
                               s.qh[:, so4 + c * CH:so4 + (c + 1) * CH],
                               start=False, stop=True)
                s.sT.inc(ins)
            self.T(omm)
            ctx[("omm", i)] = s.sT.n

        def dsmm(t, i=i, c=c, stb=stb, vsl=vsl, own=own):
            t.wait_ge(s.sS.h, ctx[("krtm", i)])
            t.wait_ge(s.sV.h, ctx.get(("stt", i), s.slot_state_free[i]))
            if i == 1 and own:
                t.wait_ge(s.sV.h, ctx[("sc", 1)])
            t.wait_ge(s.sV.h, vc_tok)
            half = (c % 2) * CH
            s.sT.inc(t.matmul(s.ps[:, stb, 0:128],
                              s.krtm[half:half + CH, i * 128:(i + 1) * 128],
                              vsl, start=True, stop=True))
        self.T(dsmm)
        tok_ds = s.sT.n
        ctx[("ds", i)] = tok_ds

        def stt(v, sd4=sd4, i=i, c=c, h=h, stb=stb, tok_ds=tok_ds, own=own):
            v.wait_ge(s.sT.h, tok_ds)
            if own:
                v.wait_ge(s.sT.h, ctx.get(("omm", i), 0))
            v.wait_ge(s.sV.h, s.sV.n)
            dec = s.dec4[:, sd4 + c:sd4 + c + 1]
            s.sV.inc(v.scalar_tensor_tensor(s.Ssb[:, h * E:(h + 1) * E],
                                            s.Ssb[:, h * E:(h + 1) * E], dec,
                                            s.ps[:, stb, 0:128], OP.mult, OP.add))
            if own:
                v.wait_ge(s.sV.h, s.sV.n)
                s.sV.inc(v.tensor_copy(s.Sb16[:, h * E:(h + 1) * E],
                                       s.Ssb[:, h * E:(h + 1) * E]))
        self.V(stt)
        ctx[("stt", i)] = s.sV.n
        if own:
            ctx[("sb16", i)] = s.sV.n

    def _o_evac(self, h, i, ctx, first, last):
        self.tag = "oev"
        s = self
        obk = 4 + i

        sV_now = s.sV.n

        def ev(sc, h=h, i=i, obk=obk, sV_now=sV_now):
            sc.wait_ge(s.iniS.h, 16)
            sc.wait_ge(s.wbS.h, s._wb2_tok)
            sc.wait_ge(s.sT.h, max(ctx[("omm", i)], s.sq_used, s.w2_used))
            sc.wait_ge(s.sV.h, sV_now)
            sc.wait_ge(s.sS.h, s.sS.n)
            sc.mul(s.ob[:, h * SUB:(h + 1) * SUB], s.ps[:, obk, :], s.nwsb[:, h:h + 1])
            s.sS.inc(sc.activation(s.sqb[:, :], s.ps[:, obk, :],
                                   AF.Square))
        self.S(ev)
        s.ob_free[i] = s.sS.n
        tok_ev = s.sS.n

        def ones(t, i=i, tok_ev=tok_ev, first=first, last=last):
            t.wait_ge(s.sS.h, tok_ev)
            s.sT.inc(t.matmul(s.ps[0:1, 6, :], s.onec[:, :],
                              s.sqb[:, :], start=first, stop=last))
        self.T(ones)
        s.sq_used = s.sT.n
        self._rms_tok = s.sT.n

    def _rms_wo(self, sidx, buf):
        self.tag = "wo"
        s = self
        tok_sum = self._rms_tok
        obS = s.sS.n

        def rstd(sc, tok_sum=tok_sum):
            sc.wait_ge(s.sT.h, tok_sum)
            sc.wait_ge(s.sS.h, s.sS.n)
            s.sS.inc(sc.activation(s.rr[0:1, :], s.ps[0:1, 6, :], AF.Ln,
                                   scale=1.0 / D, bias=EPS_RMS))
            sc.wait_ge(s.sS.h, s.sS.n)
            s.sS.inc(sc.activation(s.rr[0:1, :], s.rr[0:1, :], AF.Exp, scale=-0.5))
        self.S(rstd)
        tok_rr = s.sS.n

        def rtr(t, tok_rr=tok_rr):
            t.wait_ge(s.sS.h, tok_rr)
            for tb in range(NTB):
                ins = t.transpose(s.psb[:, 640 + 4 * tb:640 + 4 * tb + 1],
                                  s.rr[0:1, tb * 128:(tb + 1) * 128], s.id1[:, :])
            s.sT.inc(ins)
        self.T(rtr)
        tok_rtr = s.sT.n

        def rev(v, tok_rtr=tok_rtr):
            v.wait_ge(s.sT.h, tok_rtr)
            v.wait_ge(s.sV.h, s.sV.n)
            s.sV.inc(v.tensor_copy(s.rcol[:, :], s.psb[:, 640:656:4]))
        self.V(rev)
        tok_rcol = s.sV.n

        for g in range(4):
            for half in range(2):
                wbuf = (2 * g + half) % 2

                def ld(gp, g=g, half=half, wbuf=wbuf):
                    gp.wait_ge(s.sT.h, max(s.wo_used[wbuf], s.wv_used[wbuf]))
                    gp.wait_ge(s.wbS.h, s._wb2_tok)
                    s.woS[wbuf].pre(gp)
                    s.woS[wbuf].dma(gp.dma_start(s.wC[:, wbuf * 4096:(wbuf + 1) * 4096],
                                                 s.wo_t[g, half, :, :]))
                self.G(ld)
                tok_w = s.woS[wbuf].n

                def mm(t, g=g, half=half, wbuf=wbuf, tok_w=tok_w, obS=obS):
                    t.wait_ge(s.woS[wbuf].h, tok_w)
                    t.wait_ge(s.sS.h, obS)
                    if half == 0:
                        t.wait_ge(s.sV.h, s.sV.n)
                    for f8 in range(8):
                        f = half * 8 + f8
                        for tb in range(NTB):
                            ins = t.matmul(
                                s.ps[:, 2 + tb, :],
                                s.ob[:, f * SUB + tb * 128:f * SUB + (tb + 1) * 128],
                                s.wC[:, wbuf * 4096 + f8 * 512:wbuf * 4096 + (f8 + 1) * 512],
                                start=(f == 0), stop=(f == 15))
                    s.sT.inc(ins)
                self.T(mm)
                s.wo_used[wbuf] = s.sT.n
            tok_mm = s.sT.n

            def ev(v, g=g, buf=buf, tok_mm=tok_mm, tok_rcol=tok_rcol):
                v.wait_ge(s.sT.h, tok_mm)
                v.wait_ge(s.sV.h, tok_rcol)
                v.wait_ge(s.sV.h, s.sV.n)
                for tb in range(NTB):
                    x_sl = s.xln[:, buf * 4 * D + tb * D + g * 512:
                                 buf * 4 * D + tb * D + (g + 1) * 512]
                    ins = v.scalar_tensor_tensor(
                        s.y1g[:, tb * D + g * 512:tb * D + (g + 1) * 512],
                        s.ps[:, 2 + tb, :], s.rcol[:, tb:tb + 1], x_sl, OP.mult, OP.add)
                s.sV.inc(ins)
            self.V(ev)
        s.y1_tok = s.sV.n

    def _ffn(self, sidx):
        self.tag = "ffn_gate"
        s = self
        fm_tok, _ = self._ln_into_fm0(True, s.y1_tok)

        def gmm(t, fm_tok=fm_tok):
            t.wait_ge(s.iniS.h, 48)
            t.wait_ge(s.sS.h, fm_tok)
            t.wait_ge(s.sV.h, s.sV.n)
            for tb in range(NTB):
                for k in range(16):
                    ins = t.matmul(s.ps[:, tb, 0:8],
                                   s.fm0[:, k * SUB + tb * 128:k * SUB + (tb + 1) * 128],
                                   s.gwsb[:, k * 8:(k + 1) * 8],
                                   start=(k == 0), stop=(k == 15))
                s.sT.inc(ins)
        self.T(gmm)
        tok_g = s.sT.n

        for tb in range(NTB):
            def pre(v, tb=tb, tok_g=tok_g):
                v.wait_ge(s.sT.h, tok_g)
                v.wait_ge(s.sV.h, s.sV.n)
                lg = s.smw[:, 0:8]
                mx = s.smw[:, 8:9]
                s.sV.inc(v.tensor_copy(lg, s.ps[:, tb, 0:8]))
                v.wait_ge(s.sV.h, s.sV.n)
                s.sV.inc(v.reduce_max(mx, lg, axis=AX.X))
                v.wait_ge(s.sV.h, s.sV.n)
                s.sV.inc(v.tensor_scalar(lg, lg, mx, None, OP.subtract))
            self.V(pre)
            tok_p = s.sV.n

            def ex(sc, tok_p=tok_p):
                sc.wait_ge(s.sV.h, tok_p)
                sc.wait_ge(s.sS.h, s.sS.n)
                s.sS.inc(sc.activation(s.smw[:, 0:8], s.smw[:, 0:8], AF.Exp))
            self.S(ex)
            tok_e = s.sS.n

            def sm(v, tb=tb, tok_e=tok_e):
                v.wait_ge(s.sS.h, tok_e)
                v.wait_ge(s.sV.h, s.sV.n)
                lg = s.smw[:, 0:8]
                sm_ = s.smw[:, 9:10]
                m1 = s.smw[:, 10:11]
                iv = s.smw[:, 11:12]
                m2 = s.smw[:, 12:13]
                cb = s.comb[:, tb * 8:(tb + 1) * 8]
                v.reduce_sum(sm_, lg, axis=AX.X)
                s.sV.inc(v.reduce_max(m1, lg, axis=AX.X))
                v.wait_ge(s.sV.h, s.sV.n)
                v.reciprocal(iv, sm_)
                s.sV.inc(v.tensor_scalar(cb, lg, m1, None, OP.is_ge))
                v.wait_ge(s.sV.h, s.sV.n)
                s.sV.inc(v.scalar_tensor_tensor(cb, cb, -1e30, lg, OP.mult, OP.add))
                v.wait_ge(s.sV.h, s.sV.n)
                s.sV.inc(v.reduce_max(m2, cb, axis=AX.X))
                v.wait_ge(s.sV.h, s.sV.n)
                s.sV.inc(v.tensor_scalar(cb, lg, m2, None, OP.is_ge))
                v.wait_ge(s.sV.h, s.sV.n)
                s.sV.inc(v.tensor_tensor(cb, cb, lg, OP.mult))
                v.wait_ge(s.sV.h, s.sV.n)
                s.sV.inc(v.tensor_scalar(cb, cb, iv, None, OP.mult))
            self.V(sm)
        tok_comb = s.sV.n

        self._ldw2(0)
        for e in range(NEXP):
            self._gphase(e, fm_tok, tok_comb)
            if e > 0:
                self._w2phase(e - 1, tok_comb)
        self._w2phase(NEXP - 1, tok_comb)
        s.wv_used = [s.w2_used, s.w2_used]
        s.projbank_free = [s._gp_tok, s._gp_tok]
        s.projbank_freeS = [s._silu_tok, s._silu_tok]
        s.ffn_pass += 1

        tok_y = s.sV.n

        def st(sy, sidx=sidx, tok_y=tok_y):
            sy.wait_ge(s.sV.h, tok_y)
            s.outS.pre(sy)
            s.outS.dma(sy.dma_start(
                s.out[sidx * SUB:(sidx + 1) * SUB, :].rearrange("(a p) d -> p a d", p=128),
                s.yac[:, :].rearrange("p (a d) -> p a d", d=D)))
        self.SY(st)
        s.yac_free = s.outS.n

    def _w2sl(self, ic):
        s = self
        if ic < 4:
            return s.ob[:, ic * 2048:(ic + 1) * 2048]
        return s.wC[:, (ic - 4) * 2048:(ic - 3) * 2048]

    def _ldw2(self, e):
        s = self

        first = s.ffn_pass == 0

        stream = s.w2S if first else s.w2Sh

        def ldw2(gp, e=e, first=first, stream=stream):
            gp.wait_ge(s.sT.h, max(s.w2_used, s.wo_used[0], s.wo_used[1],
                                   s.wv_used[0], s.wv_used[1]))
            if first:
                gp.wait_ge(s.wbS.h, s._wb2_tok)
            else:
                gp.wait_ge(s.w2S.h, s.w2S.n)
            stream.pre(gp)
            for ic in range(8):
                src2 = s.w2_t[e, ic, :, :] if first else s.w2c[e, ic, :, :]
                stream.dma(gp.dma_start(self._w2sl(ic), src2))
        self.G(ldw2) if first else self.SY(ldw2)
        s._w2_ldtok[e] = (stream, stream.n)
        if first:
            tok_ld = s.w2S.n

            def wb2(sy, e=e, tok_ld=tok_ld):
                sy.wait_ge(s.w2S.h, tok_ld)
                s.wbS.pre(sy)
                for ic in range(8):
                    s.wbS.dma(sy.dma_start(s.w2c[e, ic, :, :], self._w2sl(ic)))
                s._wb2_tok = s.wbS.n
            self.SY(wb2)

    def _gphase(self, e, fm_tok, tok_comb):
        s = self
        self.tag = "expert_g"
        gbase = (e % 2) * 4096
        for ic in range(8):
            wbuf = ic % 2

            first = s.ffn_pass == 0

            stream = s.w13S[wbuf] if first else s.w13Sh[wbuf]

            def ld13(gp, e=e, ic=ic, wbuf=wbuf, first=first, stream=stream):
                gp.wait_ge(s.sT.h, max(s.w13_used[wbuf], s.wk_used[wbuf], s.wq_used[wbuf]))
                if first:
                    gp.wait_ge(s.wbS.h, s._wb_tok[wbuf])
                else:
                    gp.wait_ge(s.w13S[wbuf].h, s.w13S[wbuf].n)
                stream.pre(gp)
                src1 = s.w1_t[e, ic, :, :] if first else s.w1c[e, ic, :, :]
                src3 = s.w3_t[e, ic, :, :] if first else s.w3c[e, ic, :, :]
                stream.dma(gp.dma_start(s.wA[:, wbuf * D:(wbuf + 1) * D], src1))
                stream.dma(gp.dma_start(s.wB[:, wbuf * D:(wbuf + 1) * D], src3))
            self.G(ld13) if first else self.SY(ld13)
            tok_w = (stream, stream.n)
            if first:
                def wb13(sy, e=e, ic=ic, wbuf=wbuf, tok_w=tok_w):
                    sy.wait_ge(tok_w[0].h, tok_w[1])
                    s.wbS.pre(sy)
                    s.wbS.dma(sy.dma_start(s.w1c[e, ic, :, :], s.wA[:, wbuf * D:(wbuf + 1) * D]))
                    s.wbS.dma(sy.dma_start(s.w3c[e, ic, :, :], s.wB[:, wbuf * D:(wbuf + 1) * D]))
                    s._wb_tok[wbuf] = s.wbS.n
                self.SY(wb13)

            def hmm(t, ic=ic, wbuf=wbuf, tok_w=tok_w, fm_tok=fm_tok,
                    silu_tok=getattr(s, "_silu_tok", 0), gp_tok=getattr(s, "_gp_tok", 0),
                    comb_tok=tok_comb):
                t.wait_ge(tok_w[0].h, tok_w[1])
                t.wait_ge(s.sS.h, max(fm_tok, silu_tok))
                t.wait_ge(s.sV.h, comb_tok)
                for k in range(16):
                    ins = t.matmul(s.ps[:, 0, :],
                                   s.wA[:, wbuf * D + k * 128:wbuf * D + (k + 1) * 128],
                                   s.fm0[:, k * SUB:(k + 1) * SUB], start=(k == 0), stop=(k == 15))
                s.sT.inc(ins)
                t.wait_ge(s.sV.h, gp_tok)
                for k in range(16):
                    ins = t.matmul(s.ps[:, 1, :],
                                   s.wB[:, wbuf * D + k * 128:wbuf * D + (k + 1) * 128],
                                   s.fm0[:, k * SUB:(k + 1) * SUB], start=(k == 0), stop=(k == 15))
                s.sT.inc(ins)
            self.T(hmm)
            s.w13_used[wbuf] = s.sT.n
            tok_h = s.sT.n

            def sl(sc, tok_h=tok_h - 1):
                sc.wait_ge(s.sT.h, tok_h)
                sc.wait_ge(s.sS.h, s.sS.n)
                sc.wait_ge(s.sV.h, s.sV.n)
                s.sS.inc(sc.activation(s.sqb[:, :], s.ps[:, 0, :], AF.Silu))
            self.S(sl)
            s._silu_tok = s.sS.n
            tok_sl = s.sS.n

            def gp_(v, e=e, ic=ic, gbase=gbase, tok_sl=tok_sl, tok_h=tok_h):
                v.wait_ge(s.sS.h, tok_sl)
                v.wait_ge(s.sT.h, max(tok_h, s.w2_used))
                v.wait_ge(s.sV.h, s.sV.n)
                s.sV.inc(v.tensor_tensor(s.vcb[:, gbase + ic * SUB:gbase + (ic + 1) * SUB],
                                         s.sqb[:, :], s.ps[:, 1, :], OP.mult))
            self.V(gp_)
            s._gp_tok = s.sV.n
        s._g_done_prev, s._g_done = s._g_done, s.sV.n

    def _w2phase(self, e, tok_comb):
        s = self
        self.tag = "expert_w2"
        gbase = (e % 2) * 4096
        g_done = s._g_done if e == NEXP - 1 else s._g_done_prev
        for tb in range(NTB):
            def w2mm(t, e=e, tb=tb, gbase=gbase, g_done=g_done):
                t.wait_ge(s._w2_ldtok[e][0].h, s._w2_ldtok[e][1])
                t.wait_ge(s.sV.h, g_done)
                for ic in range(8):
                    src = self._w2sl(ic)
                    for g in range(4):
                        if ic == 0:
                            t.wait_ge(s.sS.h, s._ycp_tok[g])
                        ins = t.matmul(
                            s.ps[:, 2 + g, :],
                            s.vcb[:, gbase + ic * SUB + tb * 128:gbase + ic * SUB + (tb + 1) * 128],
                            src[:, g * 512:(g + 1) * 512],
                            start=(ic == 0), stop=(ic == 7))
                s.sT.inc(ins)
            self.T(w2mm)
            s.w2_used = s.sT.n
            tok_mm = s.sT.n

            for half in range(2):
                def ycp(sc, tb=tb, half=half, tok_mm=tok_mm):
                    sc.wait_ge(s.sT.h, tok_mm)
                    v_tok = s._yst_tok[half]
                    sc.wait_ge(s.sV.h, v_tok)
                    for gg in range(2):
                        g = half * 2 + gg
                        sc.wait_ge(s.sS.h, s.sS.n)
                        ins = sc.copy(s.kdec[:, gg * 512:(gg + 1) * 512] if half == 0 else
                                      s.krem[:, gg * 512:(gg + 1) * 512], s.ps[:, 2 + g, :])
                        s.sS.inc(ins)
                        s._ycp_tok[g] = s.sS.n
                self.S(ycp)

                def yst(v, e=e, tb=tb, half=half):
                    v.wait_ge(s.sS.h, s._ycp_tok[half * 2 + 1])
                    v.wait_ge(s.sV.h, max(tok_comb, s.sV.n))
                    if e == 0 and tb == 0 and half == 0:
                        v.wait_ge(s.outS.h, s.yac_free)
                    stg = s.kdec[:, 0:1024] if half == 0 else s.krem[:, 0:1024]
                    ya = s.yac[:, tb * D + half * 1024:tb * D + (half + 1) * 1024]
                    if e == 0:
                        ins = v.scalar_tensor_tensor(ya, stg,
                                                     s.comb[:, tb * 8 + e:tb * 8 + e + 1],
                                                     s.y1g[:, tb * D + half * 1024:tb * D + (half + 1) * 1024],
                                                     OP.mult, OP.add)
                    elif e < 8:
                        ins = v.scalar_tensor_tensor(ya, stg,
                                                     s.comb[:, tb * 8 + e:tb * 8 + e + 1],
                                                     ya, OP.mult, OP.add)
                    else:
                        ins = v.tensor_tensor(ya, stg, ya, OP.add)
                    s.sV.inc(ins)
                    s._yst_tok[half] = s.sV.n
                self.V(yst)
        if e + 1 < NEXP:
            self._ldw2(e + 1)


def _tile_weights(inp):
    Wq = np.asarray(inp["L0_attn_Wq"])
    Wk = np.asarray(inp["L0_attn_Wk"])
    Wv = np.asarray(inp["L0_attn_Wv"])
    Wo = np.asarray(inp["L0_attn_Wo"])
    nw = np.asarray(inp["L0_attn_norm_w"])
    gw = np.asarray(inp["L0_ffn_gate_w"])
    w1 = np.asarray(inp["L0_ffn_w1"])
    w2 = np.asarray(inp["L0_ffn_w2"])
    w3 = np.asarray(inp["L0_ffn_w3"])
    ws1 = np.asarray(inp["L0_ffn_ws1"])
    ws2 = np.asarray(inp["L0_ffn_ws2"])
    ws3 = np.asarray(inp["L0_ffn_ws3"])

    def qk_tile(W):
        return np.ascontiguousarray(
            W.reshape(H, 128, 16, 128).transpose(0, 3, 2, 1).reshape(H, 128, D))

    wq_t = qk_tile(Wq)
    wk_t = qk_tile(Wk)
    wv_t = np.ascontiguousarray(
        Wv.reshape(8, 256, 16, 128).transpose(0, 3, 2, 1).reshape(8, 128, 16 * 256))
    wo_t = np.ascontiguousarray(
        Wo.reshape(4, 512, 2, 8, 128).transpose(0, 2, 4, 3, 1).reshape(4, 2, 128, 8 * 512))
    nw_t = np.ascontiguousarray(nw.reshape(16, 128).T)
    gw_t = np.ascontiguousarray(gw.reshape(8, 16, 128).transpose(2, 1, 0).reshape(128, 16 * 8))

    w1all = np.concatenate([w1, ws1[None]], axis=0)
    w3all = np.concatenate([w3, ws3[None]], axis=0)
    w2all = np.concatenate([w2, ws2[None]], axis=0)
    w13 = lambda w: np.ascontiguousarray(
        w.reshape(NEXP, 8, 128, 16, 128).transpose(0, 1, 4, 3, 2).reshape(NEXP, 8, 128, 16 * 128))
    w1_t = w13(w1all)
    w3_t = w13(w3all)
    w2_t = np.ascontiguousarray(w2all.transpose(0, 2, 1).reshape(NEXP, 8, 128, D))
    return dict(wq_t=wq_t, wk_t=wk_t, wv_t=wv_t, wo_t=wo_t, nw_t=nw_t, gw_t=gw_t,
                w1_t=w1_t, w3_t=w3_t, w2_t=w2_t)


def make_in_maps(inp):
    x = np.asarray(inp["L0_attn_x"])
    w = _tile_weights(inp)
    ones = np.ones((128, 1), np.float32)
    zeros = np.zeros((128, 1), np.float32)
    in_maps = []
    for core in range(8):
        b, half = core // 2, core % 2
        m = dict(w)
        m["x_own"] = np.ascontiguousarray(x[b, half * TOK:(half + 1) * TOK, :])
        m["x_pre"] = np.ascontiguousarray(x[b, 0:TOK, :])
        m["pre_sc"] = ones if half == 1 else zeros
        in_maps.append(m)
    return in_maps


def kernel(**inp):
    in_maps = make_in_maps(inp)
    nc = Builder().build()
    res = run_bass_kernel_spmd(nc, in_maps, core_ids=list(range(8)))
    out = np.empty((4, 4096, D), np.float32)
    for core in range(8):
        b, half = core // 2, core % 2
        out[b, half * TOK:(half + 1) * TOK, :] = res.results[core]["out"]
    return out


if __name__ == "__main__":
    nc = Builder().build()
    print("instructions:", sum(1 for _ in nc.all_instructions()))


# version 06
# speedup vs baseline: 1.0341x; 1.0341x over previous
import numpy as np
from contextlib import ExitStack

import concourse.bass as bass
import concourse.mybir as mybir
from concourse.bass_utils import run_bass_kernel_spmd
from concourse.alu_op_type import AluOpType as OP

F32 = mybir.dt.float32
BF16 = mybir.dt.bfloat16
AF = mybir.ActivationFunctionType
AX = mybir.AxisListType

D = 2048
H = 16
E = 128
CH = 64
TOK = 2048
SUB = 512
NSUB = TOK // SUB
NTB = SUB // 128
NCH = SUB // CH
INTER = 1024
NEXP = 9
EPS_LN = 1e-5
EPS_RMS = 4e-6


class Tok:
    def __init__(self, ctx, nc, name):
        self.h = ctx.enter_context(nc.semaphore(name))
        self.n = 0

    def inc(self, instr, k=1):
        instr.then_inc(self.h, k)
        self.n += k
        return self.n


class DStream(Tok):

    def pre(self, eng):
        eng.wait_ge(self.h, self.n)

    def dma(self, instr):
        instr.then_inc(self.h, 16)
        self.n += 16
        return self.n


class RecEngine:

    def __init__(self, tagf=None):
        self.ops = []
        self.tagf = tagf or (lambda: "")

    def __getattr__(self, name):
        if name.startswith("_"):
            raise AttributeError(name)
        ops = self.ops
        tagf = self.tagf

        def call(*a, **k):
            entry = (name, a, k, [], tagf())
            ops.append(entry)

            class _I:
                def then_inc(self, sem, val):
                    entry[3].append((sem, val))
                    return self
            return _I()
        return call

    def replay(self, eng):
        for name, a, k, incs, _tag in self.ops:
            ins = getattr(eng, name)(*a, **k)
            for sem, val in incs:
                ins.then_inc(sem, val)


class Builder:
    def __init__(self, dbg=False):
        self.dbg = dbg
        nc = self.nc = bass.Bass("TRN2", target_bir_lowering=False)
        ap = lambda t: t.ap()
        self.x_own = ap(nc.dram_tensor("x_own", [TOK, D], F32, kind="ExternalInput"))
        self.x_pre = ap(nc.dram_tensor("x_pre", [TOK, D], F32, kind="ExternalInput"))
        self.pre_sc = ap(nc.dram_tensor("pre_sc", [128, 1], F32, kind="ExternalInput"))
        self.wq_t = ap(nc.dram_tensor("wq_t", [H, 128, D], F32, kind="ExternalInput"))
        self.wk_t = ap(nc.dram_tensor("wk_t", [H, 128, D], F32, kind="ExternalInput"))
        self.wv_t = ap(nc.dram_tensor("wv_t", [8, 128, 16 * 256], F32, kind="ExternalInput"))
        self.wo_t = ap(nc.dram_tensor("wo_t", [4, 2, 128, 8 * 512], F32, kind="ExternalInput"))
        self.nw_t = ap(nc.dram_tensor("nw_t", [128, 16], F32, kind="ExternalInput"))
        self.gw_t = ap(nc.dram_tensor("gw_t", [128, 16 * 8], F32, kind="ExternalInput"))
        self.w1_t = ap(nc.dram_tensor("w1_t", [NEXP, 8, 128, 16 * 128], F32, kind="ExternalInput"))
        self.w3_t = ap(nc.dram_tensor("w3_t", [NEXP, 8, 128, 16 * 128], F32, kind="ExternalInput"))
        self.w2_t = ap(nc.dram_tensor("w2_t", [NEXP, 8, 128, D], F32, kind="ExternalInput"))
        self.out = ap(nc.dram_tensor("out", [TOK, D], F32, kind="ExternalOutput"))
        self.w1c = ap(nc.dram_tensor("w1c", [NEXP, 8, 128, 16 * 128], BF16, kind="Internal"))
        self.w3c = ap(nc.dram_tensor("w3c", [NEXP, 8, 128, 16 * 128], BF16, kind="Internal"))
        self.w2c = ap(nc.dram_tensor("w2c", [NEXP, 8, 128, D], BF16, kind="Internal"))

        self.ctx = ExitStack()
        en = self.ctx.enter_context
        sb = lambda name, shp, dt: en(nc.sbuf_tensor(name, shp, dt))
        self.y1g = sb("y1g", [128, 4 * D], BF16)
        self.xln = sb("xln", [128, 2 * 4 * D], BF16)
        self.yac = sb("yac", [128, 4 * D], F32)
        self.fm0 = sb("fm0", [128, 16 * SUB], BF16)
        self.vcb = sb("vcb", [128, 4 * D], BF16)
        self.ob = sb("ob", [128, H * SUB], BF16)
        self.wA = sb("wA", [128, 2 * D], BF16)
        self.wB = sb("wB", [128, 2 * D], BF16)
        self.wC = sb("wC", [128, 2 * 4096], BF16)
        self.th2 = sb("th2", [128, 2 * SUB], F32)
        self.usb = sb("usb", [128, 2 * SUB], F32)
        self.f2s = sb("f2s", [128, 2 * SUB], F32)
        self.Ep = sb("Ep", [128, 2 * SUB], F32)
        self.Em = sb("Em", [128, 2 * SUB], BF16)
        self.th1 = sb("th1", [128, 2 * SUB], BF16)
        self.qh = sb("qh", [128, 4 * SUB], BF16)
        self.kdec = self.Em
        self.krem = sb("krem", [128, 4 * SUB], BF16)
        self.dec4 = sb("dec4", [128, 4 * NCH], F32)
        self.krtm = sb("krtm", [128, 4 * 128], BF16)
        self.scb = sb("scb", [128, 2 * 256], BF16)
        self.Ssb = sb("Ssb", [128, H * E], F32)
        self.Sb16 = sb("Sb16", [128, H * E], BF16)
        self.sqb = sb("sqb", [128, SUB], BF16)
        self.umask = sb("umask", [64, SUB], BF16)
        self.iden = sb("iden", [128, 128], BF16)
        self.id1 = sb("id1", [1, 1], BF16)
        self.onec = sb("onec", [128, 1], BF16)
        self.onech = sb("onech", [128, CH], F32)
        self.nwsb = sb("nwsb", [128, 16], F32)
        self.gwsb = sb("gwsb", [128, 16 * 8], BF16)
        self.psc = sb("psc", [128, 1], F32)
        self.bn24 = sb("bn24", [128, 24], F32)
        self.lnst = sb("lnst", [128, 16], F32)
        self.rr = sb("rr", [1, SUB], BF16)
        self.rcol = sb("rcol", [128, 4], F32)
        self.comb = sb("comb", [128, 4 * 8], F32)
        self.smw = sb("smw", [128, 16], F32)
        self.cst = sb("cst", [128, 8], F32)
        self.ps = en(nc.psum_tensor("ps", [128, 7, 512], F32))
        self.psb = en(nc.psum_tensor("psb", [128, 1024], BF16))
        self.XTO = 3 * D

        self.CONSTS = [0.5, 0.25, -0.5, EPS_LN, EPS_RMS, 1.0 / D, 1.0, 256.0]
        for i, v in enumerate(self.CONSTS):
            nc.const_aps.register(F32, float(v), self.cst[:, i:i + 1])

        c = self.ctx
        self.sT = Tok(c, nc, "sT")
        self.sV = Tok(c, nc, "sV")
        self.sS = Tok(c, nc, "sS")
        self.sG = Tok(c, nc, "sG")
        self.xS = [DStream(c, nc, "xS0"), DStream(c, nc, "xS1")]
        self.wkS = [DStream(c, nc, "wkS0"), DStream(c, nc, "wkS1")]
        self.wqS = [DStream(c, nc, "wqS0"), DStream(c, nc, "wqS1")]
        self.wvS = [DStream(c, nc, "wvS0"), DStream(c, nc, "wvS1")]
        self.woS = [DStream(c, nc, "woS0"), DStream(c, nc, "woS1")]
        self.w13S = [DStream(c, nc, "w13S0"), DStream(c, nc, "w13S1")]
        self.w2S = DStream(c, nc, "w2S")
        self.outS = DStream(c, nc, "outS")
        self.iniS = DStream(c, nc, "iniS")
        self.wbS = DStream(c, nc, "wbS")
        self.w2Sh = DStream(c, nc, "w2Sh")
        self.w13Sh = [DStream(c, nc, "w13Sh0"), DStream(c, nc, "w13Sh1")]
        self.ffn_pass = 0

        self.xtok = {}
        self.subcnt = 0
        self.wv_used = [0, 0]
        self.wk_used = [0, 0]
        self.wq_used = [0, 0]
        self.wo_used = [0, 0]
        self.w13_used = [0, 0]
        self.w2_used = 0
        self.projbank_free = [0, 0]
        self.projbank_freeS = [0, 0]
        self.sc_free = 0
        self.ob_free = [0, 0]
        self.sq_used = 0
        self.p_stt = {}
        self.p_sb16 = {}
        self.slot_state_free = [0, 0]
        self.slot_state_free2 = {}
        self.chain_free = [0, 0]
        self.scb_used = [0, 0]
        self.yac_free = 0
        self._silu_tok = 0
        self._gp_tok = 0
        self._g_done = 0
        self._g_done_prev = 0
        self._yst_tok = [0, 0]
        self._ycp_tok = [0, 0, 0, 0]
        self._w2_ldtok = {}
        self.w2_tok = 0
        self._wb_tok = [0, 0]
        self._wb2_tok = 0
        self.y1_tok = 0

    def T(self, f): f(self.rec["tensor"])
    def V(self, f): f(self.rec["vector"])
    def S(self, f): f(self.rec["scalar"])
    def G(self, f): f(self.rec["gpsimd"])
    def SY(self, f): f(self.rec["sync"])

    def build(self):
        nc = self.nc
        self.tag = "init"
        self.rec = {k: RecEngine(lambda: self.tag) for k in ("tensor", "vector", "scalar", "gpsimd", "sync")}
        self._init()
        for sidx in range(NSUB):
            self._attn_subtile(sidx, own=False)
        self._sb16_refresh()
        for sidx in range(NSUB):
            self._attn_subtile(sidx, own=True)
            self._ffn(sidx)
        s = self
        self.SY(lambda sy: sy.wait_ge(s.outS.h, s.outS.n))
        with self.ctx:
            with nc.Block() as blk:
                blk.tensor(self.rec["tensor"].replay)
                blk.vector(self.rec["vector"].replay)
                blk.scalar(self.rec["scalar"].replay)
                blk.gpsimd(self.rec["gpsimd"].replay)
                blk.sync(self.rec["sync"].replay)
        return nc

    def _init(self):
        s = self

        def vinit(v):
            for i, val in enumerate(s.CONSTS):
                v.memset(s.cst[:, i:i + 1], float(val))
            v.memset(s.onec[:, :], 1.0)
            v.memset(s.onech[:, :], 1.0)
            v.memset(s.Ssb[:, :], 0.0)
            s.sV.inc(v.memset(s.Sb16[:, :], 0.0))
        self.V(vinit)

        def ginit(g):
            g.iota(s.f2s[0:64, 0:SUB], [[0, NCH], [1, CH]], channel_multiplier=-1,
                   allow_small_or_imprecise_dtypes=True)
            g.iota(s.f2s[:, 512:640], [[1, 128]], channel_multiplier=-1,
                   allow_small_or_imprecise_dtypes=True)
            s.sG.inc(g.memset(s.id1[:, :], 1.0))
            s.iniS.pre(g)
            s.iniS.dma(g.dma_start(s.nwsb[:, :], s.nw_t[:, :]))
            s.iniS.pre(g)
            s.iniS.dma(g.dma_start(s.psc[:, :], s.pre_sc[:, :]))
            s.iniS.pre(g)
            s.iniS.dma(g.dma_start(s.gwsb[:, :], s.gw_t[:, :]))
        self.G(ginit)
        gtok = s.sG.n

        def vinit2(v):
            v.wait_ge(s.sG.h, gtok)
            v.wait_ge(s.sV.h, s.sV.n)
            v.tensor_single_scalar(s.umask[:, :], s.f2s[0:64, 0:SUB], 0.0, OP.is_ge)
            s.sV.inc(v.tensor_single_scalar(s.f2s[:, 512:640], s.f2s[:, 512:640], 0.0,
                                            OP.is_equal))
            v.wait_ge(s.sV.h, s.sV.n)
            s.sV.inc(v.tensor_copy(s.iden[:, :], s.f2s[:, 512:640]))
        self.V(vinit2)

        def xload(gp):
            for u in (0, 1):
                s.xS[u % 2].pre(gp)
                s.xS[u % 2].dma(gp.dma_start(
                    s.xln[:, (u % 2) * 4 * D:(u % 2 + 1) * 4 * D].rearrange("p (a d) -> p a d", d=D),
                    s.x_pre[u * SUB:(u + 1) * SUB, :].rearrange("(a p) d -> p a d", p=128)))
                s.xtok[u] = s.xS[u % 2].n
        self.G(xload)

    def _sb16_refresh(self):
        s = self

        def cp(v):
            v.wait_ge(s.sV.h, s.sV.n)
            s.sV.inc(v.tensor_copy(s.Sb16[:, :], s.Ssb[:, :]))
        self.V(cp)
        for h in range(H):
            s.p_sb16[h] = s.sV.n

    def _prefetch_x(self, after_v_tok):
        s = self
        u = self.subcnt + 2
        if u >= 2 * NSUB:
            return
        own2 = u >= NSUB
        src = s.x_own if own2 else s.x_pre
        r0 = (u % NSUB) * SUB
        buf = u % 2

        def ld(gp):
            gp.wait_ge(s.sV.h, after_v_tok)
            s.xS[buf].pre(gp)
            s.xS[buf].dma(gp.dma_start(
                s.xln[:, buf * 4 * D:(buf + 1) * 4 * D].rearrange("p (a d) -> p a d", d=D),
                src[r0:r0 + SUB, :].rearrange("(a p) d -> p a d", p=128)))
            s.xtok[u] = s.xS[buf].n
        self.G(ld)

    def _ln_into_fm0(self, src_y1, tok_src):
        self.tag = "ln"
        """LN (no affine) of 4 tblocks -> bf16 feature-major in fm0.
        src: xln[cur buf] (attn, tok_src = dX token) or y1g (ffn, tok_src = sV).
        Returns (S token: fm0 ready, V token: src fully read)."""
        s = self
        buf = self.subcnt % 2
        last_v = 0
        for tb in range(NTB):
            x_tb = (s.y1g[:, tb * D:(tb + 1) * D] if src_y1 else
                    s.xln[:, buf * 4 * D + tb * D:buf * 4 * D + (tb + 1) * D])

            def stats(v, x_tb=x_tb, tb=tb):
                if src_y1:
                    v.wait_ge(s.sV.h, tok_src)
                else:
                    v.wait_ge(s.xS[self.subcnt % 2].h, tok_src)
                v.wait_ge(s.sV.h, s.sV.n)
                for j in range(4):
                    ins = v.bn_stats(s.bn24[:, 6 * j:6 * (j + 1)], x_tb[:, 512 * j:512 * (j + 1)])
                s.sV.inc(ins)
                ms = s.lnst[:, 4 * tb:4 * tb + 1]
                vr = s.lnst[:, 4 * tb + 1:4 * tb + 2]
                t2 = s.lnst[:, 4 * tb + 2:4 * tb + 3]
                v.wait_ge(s.sV.h, s.sV.n)
                v.tensor_tensor(s.smw[:, 0:8], s.bn24[:, 1:24:3], s.bn24[:, 1:24:3], OP.mult)
                v.reduce_sum(ms, s.bn24[:, 1:24:3], axis=AX.X)
                s.sV.inc(v.reduce_sum(vr, s.bn24[:, 2:24:3], axis=AX.X))
                v.wait_ge(s.sV.h, s.sV.n)
                v.reduce_sum(t2, s.smw[:, 0:8], axis=AX.X)
                s.sV.inc(v.tensor_scalar(ms, ms, 0.125, None, OP.mult))
                v.wait_ge(s.sV.h, s.sV.n)
                s.sV.inc(v.scalar_tensor_tensor(vr, t2, 256.0, vr, OP.mult, OP.add))
                v.wait_ge(s.sV.h, s.sV.n)
                v.tensor_scalar(vr, vr, 1.0 / D, None, OP.mult)
                s.sV.inc(v.tensor_tensor(t2, ms, ms, OP.mult))
                v.wait_ge(s.sV.h, s.sV.n)
                s.sV.inc(v.tensor_tensor(vr, vr, t2, OP.subtract))
            self.V(stats)
            tok_var = s.sV.n

            def rstd(sc, tb=tb, tok_var=tok_var):
                sc.wait_ge(s.sV.h, tok_var)
                sc.wait_ge(s.sS.h, s.sS.n)
                vr = s.lnst[:, 4 * tb + 1:4 * tb + 2]
                t2 = s.lnst[:, 4 * tb + 2:4 * tb + 3]
                rs = s.lnst[:, 4 * tb + 3:4 * tb + 4]
                s.sS.inc(sc.activation(t2, vr, AF.Ln, bias=EPS_LN))
                sc.wait_ge(s.sS.h, s.sS.n)
                s.sS.inc(sc.activation(rs, t2, AF.Exp, scale=-0.5))
            self.S(rstd)
            tok_rs = s.sS.n

            def norm(v, x_tb=x_tb, tb=tb, tok_rs=tok_rs):
                v.wait_ge(s.sS.h, tok_rs)
                v.wait_ge(s.sT.h, s.sT.n)
                v.wait_ge(s.sV.h, s.sV.n)
                ms = s.lnst[:, 4 * tb:4 * tb + 1]
                rs = s.lnst[:, 4 * tb + 3:4 * tb + 4]
                s.sV.inc(v.tensor_scalar(s.vcb[:, s.XTO:s.XTO + D], x_tb, ms, rs, OP.subtract, OP.mult))
            self.V(norm)
            tok_xn = s.sV.n
            last_v = tok_xn

            for g4 in range(4):
                def tr(t, g4=g4, tok_xn=tok_xn):
                    t.wait_ge(s.sV.h, tok_xn)
                    t.wait_ge(s.sS.h, s.sS.n)
                    for kk in range(4):
                        k = g4 * 4 + kk
                        ins = t.transpose(s.psb[:, kk * 128:(kk + 1) * 128],
                                          s.vcb[:, s.XTO + k * 128:s.XTO + (k + 1) * 128], s.iden[:, :])
                    s.sT.inc(ins)
                self.T(tr)
                tok_tr = s.sT.n

                def ev(sc, tb=tb, g4=g4, tok_tr=tok_tr):
                    sc.wait_ge(s.sT.h, tok_tr)
                    sc.wait_ge(s.sS.h, s.sS.n)
                    dst = s.fm0[:, :].rearrange("p (k t) -> p k t", t=SUB)[
                        :, 4 * g4:4 * g4 + 4, tb * 128:(tb + 1) * 128]
                    src = s.psb[:, 0:512].rearrange("p (k t) -> p k t", t=128)
                    s.sS.inc(sc.copy(dst, src))
                self.S(ev)
        return s.sS.n, last_v

    def _attn_subtile(self, sidx, own):
        s = self
        tok_x = s.xtok[self.subcnt]
        buf = self.subcnt % 2

        fm_tok, ln_v = self._ln_into_fm0(False, tok_x)
        vc_tok = self._vproj(fm_tok, own)
        if not own:
            self._prefetch_x(ln_v)
        self._pctx = {}
        self._run_pairs(own, fm_tok, vc_tok)
        if own:
            self._rms_wo(sidx, buf)
            self._prefetch_x(s.sV.n)
        self.subcnt += 1

    def _vproj(self, fm_tok, own):
        self.tag = "vproj"
        s = self
        for g in range(8):
            wbuf = g % 2

            def ld(gp, g=g, wbuf=wbuf):
                gp.wait_ge(s.sT.h, max(s.wv_used[wbuf], s.wo_used[wbuf]))
                gp.wait_ge(s.wbS.h, s._wb2_tok)
                s.wvS[wbuf].pre(gp)
                s.wvS[wbuf].dma(gp.dma_start(s.wC[:, wbuf * 4096:(wbuf + 1) * 4096],
                                             s.wv_t[g, :, :]))
            self.G(ld)
            tok_w = s.wvS[wbuf].n

            sS_now = s.sS.n

            def mm(t, g=g, wbuf=wbuf, tok_w=tok_w, fm_tok=fm_tok, sS_now=sS_now):
                t.wait_ge(s.wvS[wbuf].h, tok_w)
                t.wait_ge(s.sS.h, max(fm_tok, sS_now))
                t.wait_ge(s.sV.h, s.sV.n)
                for tb in range(NTB):
                    for k in range(16):
                        ins = t.matmul(s.ps[:, 2 + tb, 0:256],
                                       s.fm0[:, k * SUB + tb * 128:k * SUB + (tb + 1) * 128],
                                       s.wC[:, wbuf * 4096 + k * 256:wbuf * 4096 + (k + 1) * 256],
                                       start=(k == 0), stop=(k == 15))
                    s.sT.inc(ins)
            self.T(mm)
            s.wv_used[wbuf] = s.sT.n
            tok_mm = s.sT.n

            def ev(v, g=g, tok_mm=tok_mm, own=own):
                v.wait_ge(s.sT.h, tok_mm)
                v.wait_ge(s.sV.h, s.sV.n)
                for tb in range(NTB):
                    dst = s.vcb[:, tb * D + g * 256:tb * D + (g + 1) * 256]
                    if own:
                        ins = v.tensor_copy(dst, s.ps[:, 2 + tb, 0:256])
                    else:
                        if g == 0 and tb == 0:
                            v.wait_ge(s.iniS.h, 32)
                        ins = v.tensor_scalar(dst, s.ps[:, 2 + tb, 0:256],
                                              s.psc[:, 0:1], None, OP.mult)
                s.sV.inc(ins)
            self.V(ev)
        return s.sV.n

    def _run_pairs(self, own, fm_tok, vc_tok):
        s = self
        prev = None
        for hp in range(H // 2):
            ctx = {"sS0": s.sS.n, "hp": hp}
            fill = self._pair_front(own, hp, fm_tok, ctx)
            if prev is None:
                for f in fill:
                    f()
                if own:
                    self._scores_pair(hp, ctx)
            else:
                self._pair_chunks(own, prev, fill, vc_tok)
                if own:
                    self._scores_pair(hp, ctx)
                self._pair_post(own, prev)
            self._pctx[hp] = ctx
            prev = hp
        self._pair_chunks(own, prev, [], vc_tok)
        self._pair_post(own, prev)

    def _pair_front(self, own, hp, fm_tok, ctx):
        fill = []
        for i, h in enumerate((2 * hp, 2 * hp + 1)):
            fill += self._chain_steps(own, h, i, hp, fm_tok, ctx)
        return fill

    def _chain_steps(self, own, h, i, hp, fm_tok, ctx):
        self.tag = "chain"
        s = self
        wbuf = h % 2
        pb = h % 2
        so = i * SUB
        so4 = ((hp % 2) * 2 + i) * SUB
        sd4 = ((hp % 2) * 2 + i) * NCH
        steps = []

        def ldk():
            def f(gp, h=h, wbuf=wbuf, own=own):
                gp.wait_ge(s.sT.h, max(s.wk_used[wbuf], s.w13_used[wbuf]))
                gp.wait_ge(s.wbS.h, s._wb_tok[wbuf])
                s.wkS[wbuf].pre(gp)
                s.wkS[wbuf].dma(gp.dma_start(s.wB[:, wbuf * D:(wbuf + 1) * D],
                                             s.wk_t[h, :, :]))
                if own:
                    gp.wait_ge(s.sT.h, max(s.wq_used[wbuf], s.w13_used[wbuf]))
                    s.wqS[wbuf].pre(gp)
                    s.wqS[wbuf].dma(gp.dma_start(s.wA[:, wbuf * D:(wbuf + 1) * D],
                                                 s.wq_t[h, :, :]))
            self.G(f)
            ctx[("wk", i)] = s.wkS[wbuf].n
            ctx[("wq", i)] = s.wqS[wbuf].n
        steps.append(ldk)

        def fmm():
            tok_wk = ctx[("wk", i)]

            def f(t, pb=pb, wbuf=wbuf, tok_wk=tok_wk, fm_tok=fm_tok):
                t.wait_ge(s.wkS[wbuf].h, tok_wk)
                t.wait_ge(s.sS.h, max(fm_tok, s.projbank_freeS[pb]))
                t.wait_ge(s.sV.h, s.projbank_free[pb])
                for k in range(16):
                    ins = t.matmul(s.ps[:, pb, :],
                                   s.wB[:, wbuf * D + k * 128:wbuf * D + (k + 1) * 128],
                                   s.fm0[:, k * SUB:(k + 1) * SUB],
                                   start=(k == 0), stop=(k == 15))
                s.sT.inc(ins)
            self.T(f)
            s.wk_used[wbuf] = s.sT.n
            ctx[("fmm", i)] = s.sT.n
        steps.append(fmm)

        def th1e():
            tok_fmm = ctx[("fmm", i)]

            def f(sc, so=so, pb=pb, tok_fmm=tok_fmm):
                sc.wait_ge(s.sT.h, tok_fmm)
                sc.wait_ge(s.sS.h, s.sS.n)
                s.sS.inc(sc.activation(s.th1[:, so:so + SUB], s.ps[:, pb, :],
                                       AF.Tanh, scale=0.5))
            self.S(f)
            ctx[("th1", i)] = s.sS.n
        steps.append(th1e)

        def th2u():
            def f(sc, so=so):
                sc.wait_ge(s.sS.h, s.sS.n)
                sc.activation(s.th2[:, so:so + SUB], s.th1[:, so:so + SUB], AF.Tanh,
                              scale=0.25, bias=0.25)
                s.sS.inc(sc.activation(s.usb[:, so:so + SUB], s.th1[:, so:so + SUB],
                                       AF.Exp, scale=-0.5, bias=-0.5))
            self.S(f)
            ctx[("th2", i)] = s.sS.n
        steps.append(th2u)

        if own:
            def qmm():
                tok_wq = ctx[("wq", i)]
                tok_th1 = ctx[("th1", i)]

                def f(t, pb=pb, wbuf=wbuf, tok_wq=tok_wq, tok_th1=tok_th1):
                    t.wait_ge(s.wqS[wbuf].h, tok_wq)
                    t.wait_ge(s.sS.h, tok_th1)
                    for k in range(16):
                        ins = t.matmul(s.ps[:, pb, :],
                                       s.wA[:, wbuf * D + k * 128:wbuf * D + (k + 1) * 128],
                                       s.fm0[:, k * SUB:(k + 1) * SUB],
                                       start=(k == 0), stop=(k == 15))
                    s.sT.inc(ins)
                self.T(f)
                s.wq_used[wbuf] = s.sT.n
                ctx[("qmm", i)] = s.sT.n
            steps.append(qmm)

            def th1qe():
                tok_qmm = ctx[("qmm", i)]
                tok_th2 = ctx[("th2", i)]

                def f(sc, so=so, pb=pb, tok_qmm=tok_qmm, tok_th2=tok_th2):
                    sc.wait_ge(s.sT.h, tok_qmm)
                    sc.wait_ge(s.sS.h, max(tok_th2, s.sS.n))
                    s.sS.inc(sc.activation(s.th1[:, so:so + SUB], s.ps[:, pb, :],
                                           AF.Tanh, scale=0.5))
                self.S(f)
                ctx[("th1q", i)] = s.sS.n
            steps.append(th1qe)

        def chainv():
            tok_th2 = ctx[("th2", i)]

            def f(v, so=so, so4=so4, sd4=sd4, i=i, tok_th2=tok_th2):
                v.wait_ge(s.sS.h, tok_th2)
                v.wait_ge(s.sT.h, s.chain_free[i])
                v.wait_ge(s.sV.h, s.sV.n)
                v.tensor_scalar(s.f2s[:, so:so + SUB], s.th2[:, so:so + SUB], 0.5, 0.5,
                                OP.mult, OP.add)
                s.sV.inc(v.tensor_single_scalar(s.usb[:, so:so + SUB],
                                                s.usb[:, so:so + SUB], 1.0, OP.add))
                v.wait_ge(s.sV.h, s.sV.n)
                for c in range(NCH):
                    sl = slice(so + c * CH, so + (c + 1) * CH)
                    v.tensor_tensor_scan(s.Ep[:, sl], s.f2s[:, sl], s.onech[:, :],
                                         1.0, OP.mult, OP.mult)
                for c in range(NCH):
                    sl = slice(so + c * CH, so + (c + 1) * CH)
                    ins = v.tensor_tensor_scan(s.Em[:, sl], s.usb[:, sl], s.onech[:, :],
                                               0.5, OP.mult, OP.mult)
                s.sV.inc(ins)
                v.wait_ge(s.sV.h, s.sV.n)
                v.tensor_copy(s.dec4[:, sd4:sd4 + NCH],
                              s.Ep[:, so + CH - 1:so + SUB:CH])
                s.sV.inc(v.tensor_scalar(s.f2s[:, so:so + SUB], s.th2[:, so:so + SUB],
                                         -1.0, 1.0, OP.mult, OP.add))
                v.wait_ge(s.sV.h, s.sV.n)
                s.sV.inc(v.tensor_tensor(s.kdec[:, so:so + SUB], s.f2s[:, so:so + SUB],
                                         s.Em[:, so:so + SUB], OP.mult))
                v.wait_ge(s.sV.h, s.sV.n)
                for c in range(NCH):
                    ins = v.tensor_scalar(
                        s.krem[:, so4 + c * CH:so4 + (c + 1) * CH],
                        s.kdec[:, so + c * CH:so + (c + 1) * CH],
                        s.Ep[:, so + c * CH + CH - 1:so + c * CH + CH], None, OP.mult)
                s.sV.inc(ins)
            self.V(f)
            ctx[("chain", i)] = s.sV.n
            if not own:
                s.projbank_free[pb] = s.sV.n
        steps.append(chainv)

        if own:
            def qv():
                tok_th1q = ctx[("th1q", i)]

                def f(v, so=so, so4=so4, pb=pb, tok_th1q=tok_th1q):
                    v.wait_ge(s.sS.h, tok_th1q)
                    v.wait_ge(s.sV.h, s.sV.n)
                    s.sV.inc(v.scalar_tensor_tensor(s.usb[:, so:so + SUB],
                                                    s.th1[:, so:so + SUB], 1.0,
                                                    s.ps[:, pb, :], OP.add, OP.mult))
                    v.wait_ge(s.sV.h, s.sV.n)
                    s.sV.inc(v.tensor_tensor(s.qh[:, so4:so4 + SUB],
                                             s.usb[:, so:so + SUB],
                                             s.Ep[:, so:so + SUB], OP.mult))
                self.V(f)
                s.projbank_free[pb] = s.sV.n
                ctx[("qv", i)] = s.sV.n
            steps.append(qv)
        return steps

    def _scores_pair(self, hp, ctx):
        self.tag = "scores"
        s = self
        prev_ctx = self._pctx.get(hp - 1)
        for i in range(2):
            so4 = ((hp % 2) * 2 + i) * SUB
            tok_chain = ctx[("qv", i)]

            stt1 = prev_ctx.get(("stt", 1), 0) if prev_ctx else 0

            def mm(t, so4=so4, tok_chain=tok_chain, stt1=stt1):
                t.wait_ge(s.sV.h, max(tok_chain, s.sc_free, stt1))
                for c in range(NCH):
                    ins = t.matmul(s.ps[0:64, 3, c * CH:(c + 1) * CH],
                                   s.kdec[:, (so4 % (2 * SUB)) + c * CH:
                                          (so4 % (2 * SUB)) + (c + 1) * CH],
                                   s.qh[:, so4 + c * CH:so4 + (c + 1) * CH],
                                   start=True, stop=True)
                s.sT.inc(ins)
            self.T(mm)
            tok_sc = s.sT.n
            s.chain_free[i] = s.sT.n

            def mask(v, i=i, tok_sc=tok_sc, prev_ctx=prev_ctx):
                v.wait_ge(s.sT.h, tok_sc)
                if prev_ctx is not None:
                    v.wait_ge(s.sT.h, prev_ctx.get(("omm", i), 0))
                v.wait_ge(s.sV.h, s.sV.n)
                psv = s.ps[0:64, 3, :].rearrange("p (c w) -> p c w", w=CH)
                mk = s.umask[:, :].rearrange("p (c w) -> p c w", w=CH)[:, 0:4, :]
                v.tensor_tensor(s.scb[0:64, i * 256:(i + 1) * 256].rearrange(
                    "p (c w) -> p c w", w=CH), psv[:, 0::2, :], mk, OP.mult)
                s.sV.inc(v.tensor_tensor(s.scb[64:128, i * 256:(i + 1) * 256].rearrange(
                    "p (c w) -> p c w", w=CH), psv[:, 1::2, :], mk, OP.mult))
            self.V(mask)
            s.sc_free = s.sV.n
            ctx[("sc", i)] = s.sV.n

    def _pair_chunks(self, own, hp, fill, vc_tok):
        s = self
        ctx = self._pctx[hp]
        fill = list(fill)
        nf = len(fill)
        for c in range(NCH):
            for i, h in enumerate((2 * hp, 2 * hp + 1)):
                self._chunk(own, h, i, hp, c, ctx, vc_tok)
            take = (nf * (c + 1)) // NCH - (nf * c) // NCH
            for _ in range(take):
                fill.pop(0)()
        for f in fill:
            f()

    def _pair_post(self, own, hp):
        s = self
        ctx = self._pctx[hp]
        for sl in range(4):
            if ("stt", sl) in ctx:
                s.slot_state_free2[sl] = ctx[("stt", sl)]
        for i, h in enumerate((2 * hp, 2 * hp + 1)):
            s.p_stt[h] = s.sV.n
            s.slot_state_free[i] = s.sV.n
            if own:
                s.p_sb16[h] = ctx[("sb16", i)]
        if own:
            for i, h in enumerate((2 * hp, 2 * hp + 1)):
                self._o_evac(h, i, ctx, first=(hp == 0 and i == 0),
                             last=(hp == H // 2 - 1 and i == 1))

    def _chunk(self, own, h, i, hp, c, ctx, vc_tok):
        self.tag = "chunk"
        s = self
        so4 = ((hp % 2) * 2 + i) * SUB
        sd4 = ((hp % 2) * 2 + i) * NCH
        sl = i if own else ((c // 2) % 2) * 2 + i
        stb = 2 + sl
        obk = 4 + i
        tbl = (c * CH) // 128
        pbase = (c * CH) % 128
        vsl = s.vcb[pbase:pbase + CH, tbl * D + h * E:tbl * D + (h + 1) * E]

        if c % 2 == 0:
            def tr(t, so4=so4, i=i, c=c, sl=sl):
                t.wait_ge(s.sV.h, ctx.get(("stt", sl), s.slot_state_free2.get(sl, 0)))
                t.wait_ge(s.sV.h, ctx[("chain", i)])
                t.wait_ge(s.sS.h, max(ctx.get(("krtm", sl), 0), ctx["sS0"]))
                s.sT.inc(t.transpose(s.psb[:, 512 + sl * 128:512 + (sl + 1) * 128],
                                     s.krem[:, so4 + c * CH:so4 + (c + 2) * CH],
                                     s.iden[:, :]))
            self.T(tr)
            tok_tr = s.sT.n

            def ev(sc, sl=sl, tok_tr=tok_tr):
                sc.wait_ge(s.sT.h, max(tok_tr, ctx.get(("ds", sl), 0)))
                sc.wait_ge(s.sS.h, s.sS.n)
                s.sS.inc(sc.copy(s.krtm[:, sl * 128:(sl + 1) * 128],
                                 s.psb[:, 512 + sl * 128:512 + (sl + 1) * 128]))
            self.S(ev)
            ctx[("krtm", sl)] = s.sS.n

        if own:
            def omm(t, so4=so4, i=i, c=c, h=h, obk=obk, vsl=vsl, pbase=pbase):
                t.wait_ge(s.sV.h, ctx[("sc", i)])
                t.wait_ge(s.sV.h, vc_tok)
                t.wait_ge(s.sV.h, ctx.get(("sb16", i), s.p_sb16.get(h, 0)))
                if c == 0:
                    t.wait_ge(s.sS.h, s.ob_free[i])
                sl = slice(c * CH, (c + 1) * CH)
                scsl = s.scb[pbase:pbase + CH,
                             i * 256 + (c // 2) * CH:i * 256 + (c // 2 + 1) * CH]
                t.matmul(s.ps[:, obk, sl], vsl, scsl, start=True, stop=False)
                ins = t.matmul(s.ps[:, obk, sl], s.Sb16[:, h * E:(h + 1) * E],
                               s.qh[:, so4 + c * CH:so4 + (c + 1) * CH],
                               start=False, stop=True)
                s.sT.inc(ins)
            self.T(omm)
            ctx[("omm", i)] = s.sT.n

        def dsmm(t, sl=sl, c=c, stb=stb, vsl=vsl, own=own):
            t.wait_ge(s.sS.h, ctx[("krtm", sl)])
            t.wait_ge(s.sV.h, ctx.get(("stt", sl), s.slot_state_free2.get(sl, 0)))
            if sl == 1 and own:
                t.wait_ge(s.sV.h, ctx[("sc", 1)])
            t.wait_ge(s.sV.h, vc_tok)
            half = (c % 2) * CH
            s.sT.inc(t.matmul(s.ps[:, stb, 0:128],
                              s.krtm[half:half + CH, sl * 128:(sl + 1) * 128],
                              vsl, start=True, stop=True))
        self.T(dsmm)
        tok_ds = s.sT.n
        ctx[("ds", sl)] = tok_ds

        def stt(v, sd4=sd4, i=i, c=c, h=h, stb=stb, tok_ds=tok_ds, own=own):
            v.wait_ge(s.sT.h, tok_ds)
            if own:
                v.wait_ge(s.sT.h, ctx.get(("omm", i), 0))
            v.wait_ge(s.sV.h, s.sV.n)
            dec = s.dec4[:, sd4 + c:sd4 + c + 1]
            s.sV.inc(v.scalar_tensor_tensor(s.Ssb[:, h * E:(h + 1) * E],
                                            s.Ssb[:, h * E:(h + 1) * E], dec,
                                            s.ps[:, stb, 0:128], OP.mult, OP.add))
            if own:
                v.wait_ge(s.sV.h, s.sV.n)
                s.sV.inc(v.tensor_copy(s.Sb16[:, h * E:(h + 1) * E],
                                       s.Ssb[:, h * E:(h + 1) * E]))
        self.V(stt)
        ctx[("stt", sl)] = s.sV.n
        if own:
            ctx[("sb16", i)] = s.sV.n

    def _o_evac(self, h, i, ctx, first, last):
        self.tag = "oev"
        s = self
        obk = 4 + i

        sV_now = s.sV.n

        def ev(sc, h=h, i=i, obk=obk, sV_now=sV_now):
            sc.wait_ge(s.iniS.h, 16)
            sc.wait_ge(s.wbS.h, s._wb2_tok)
            sc.wait_ge(s.sT.h, max(ctx[("omm", i)], s.sq_used, s.w2_used))
            sc.wait_ge(s.sV.h, sV_now)
            sc.wait_ge(s.sS.h, s.sS.n)
            sc.mul(s.ob[:, h * SUB:(h + 1) * SUB], s.ps[:, obk, :], s.nwsb[:, h:h + 1])
            s.sS.inc(sc.activation(s.sqb[:, :], s.ps[:, obk, :],
                                   AF.Square))
        self.S(ev)
        s.ob_free[i] = s.sS.n
        tok_ev = s.sS.n

        def ones(t, i=i, tok_ev=tok_ev, first=first, last=last):
            t.wait_ge(s.sS.h, tok_ev)
            s.sT.inc(t.matmul(s.ps[0:1, 6, :], s.onec[:, :],
                              s.sqb[:, :], start=first, stop=last))
        self.T(ones)
        s.sq_used = s.sT.n
        self._rms_tok = s.sT.n

    def _rms_wo(self, sidx, buf):
        self.tag = "wo"
        s = self
        tok_sum = self._rms_tok
        obS = s.sS.n

        def rstd(sc, tok_sum=tok_sum):
            sc.wait_ge(s.sT.h, tok_sum)
            sc.wait_ge(s.sS.h, s.sS.n)
            s.sS.inc(sc.activation(s.rr[0:1, :], s.ps[0:1, 6, :], AF.Ln,
                                   scale=1.0 / D, bias=EPS_RMS))
            sc.wait_ge(s.sS.h, s.sS.n)
            s.sS.inc(sc.activation(s.rr[0:1, :], s.rr[0:1, :], AF.Exp, scale=-0.5))
        self.S(rstd)
        tok_rr = s.sS.n

        def rtr(t, tok_rr=tok_rr):
            t.wait_ge(s.sS.h, tok_rr)
            for tb in range(NTB):
                ins = t.transpose(s.psb[:, 640 + 4 * tb:640 + 4 * tb + 1],
                                  s.rr[0:1, tb * 128:(tb + 1) * 128], s.id1[:, :])
            s.sT.inc(ins)
        self.T(rtr)
        tok_rtr = s.sT.n

        def rev(v, tok_rtr=tok_rtr):
            v.wait_ge(s.sT.h, tok_rtr)
            v.wait_ge(s.sV.h, s.sV.n)
            s.sV.inc(v.tensor_copy(s.rcol[:, :], s.psb[:, 640:656:4]))
        self.V(rev)
        tok_rcol = s.sV.n

        for g in range(4):
            for half in range(2):
                wbuf = (2 * g + half) % 2

                def ld(gp, g=g, half=half, wbuf=wbuf):
                    gp.wait_ge(s.sT.h, max(s.wo_used[wbuf], s.wv_used[wbuf]))
                    gp.wait_ge(s.wbS.h, s._wb2_tok)
                    s.woS[wbuf].pre(gp)
                    s.woS[wbuf].dma(gp.dma_start(s.wC[:, wbuf * 4096:(wbuf + 1) * 4096],
                                                 s.wo_t[g, half, :, :]))
                self.G(ld)
                tok_w = s.woS[wbuf].n

                def mm(t, g=g, half=half, wbuf=wbuf, tok_w=tok_w, obS=obS):
                    t.wait_ge(s.woS[wbuf].h, tok_w)
                    t.wait_ge(s.sS.h, obS)
                    if half == 0:
                        t.wait_ge(s.sV.h, s.sV.n)
                    for f8 in range(8):
                        f = half * 8 + f8
                        for tb in range(NTB):
                            ins = t.matmul(
                                s.ps[:, 2 + tb, :],
                                s.ob[:, f * SUB + tb * 128:f * SUB + (tb + 1) * 128],
                                s.wC[:, wbuf * 4096 + f8 * 512:wbuf * 4096 + (f8 + 1) * 512],
                                start=(f == 0), stop=(f == 15))
                    s.sT.inc(ins)
                self.T(mm)
                s.wo_used[wbuf] = s.sT.n
            tok_mm = s.sT.n

            def ev(v, g=g, buf=buf, tok_mm=tok_mm, tok_rcol=tok_rcol):
                v.wait_ge(s.sT.h, tok_mm)
                v.wait_ge(s.sV.h, tok_rcol)
                v.wait_ge(s.sV.h, s.sV.n)
                for tb in range(NTB):
                    x_sl = s.xln[:, buf * 4 * D + tb * D + g * 512:
                                 buf * 4 * D + tb * D + (g + 1) * 512]
                    ins = v.scalar_tensor_tensor(
                        s.y1g[:, tb * D + g * 512:tb * D + (g + 1) * 512],
                        s.ps[:, 2 + tb, :], s.rcol[:, tb:tb + 1], x_sl, OP.mult, OP.add)
                s.sV.inc(ins)
            self.V(ev)
        s.y1_tok = s.sV.n

    def _ffn(self, sidx):
        self.tag = "ffn_gate"
        s = self
        fm_tok, _ = self._ln_into_fm0(True, s.y1_tok)

        def gmm(t, fm_tok=fm_tok):
            t.wait_ge(s.iniS.h, 48)
            t.wait_ge(s.sS.h, fm_tok)
            t.wait_ge(s.sV.h, s.sV.n)
            for tb in range(NTB):
                for k in range(16):
                    ins = t.matmul(s.ps[:, tb, 0:8],
                                   s.fm0[:, k * SUB + tb * 128:k * SUB + (tb + 1) * 128],
                                   s.gwsb[:, k * 8:(k + 1) * 8],
                                   start=(k == 0), stop=(k == 15))
                s.sT.inc(ins)
        self.T(gmm)
        tok_g = s.sT.n

        for tb in range(NTB):
            def pre(v, tb=tb, tok_g=tok_g):
                v.wait_ge(s.sT.h, tok_g)
                v.wait_ge(s.sV.h, s.sV.n)
                lg = s.smw[:, 0:8]
                mx = s.smw[:, 8:9]
                s.sV.inc(v.tensor_copy(lg, s.ps[:, tb, 0:8]))
                v.wait_ge(s.sV.h, s.sV.n)
                s.sV.inc(v.reduce_max(mx, lg, axis=AX.X))
                v.wait_ge(s.sV.h, s.sV.n)
                s.sV.inc(v.tensor_scalar(lg, lg, mx, None, OP.subtract))
            self.V(pre)
            tok_p = s.sV.n

            def ex(sc, tok_p=tok_p):
                sc.wait_ge(s.sV.h, tok_p)
                sc.wait_ge(s.sS.h, s.sS.n)
                s.sS.inc(sc.activation(s.smw[:, 0:8], s.smw[:, 0:8], AF.Exp))
            self.S(ex)
            tok_e = s.sS.n

            def sm(v, tb=tb, tok_e=tok_e):
                v.wait_ge(s.sS.h, tok_e)
                v.wait_ge(s.sV.h, s.sV.n)
                lg = s.smw[:, 0:8]
                sm_ = s.smw[:, 9:10]
                m1 = s.smw[:, 10:11]
                iv = s.smw[:, 11:12]
                m2 = s.smw[:, 12:13]
                cb = s.comb[:, tb * 8:(tb + 1) * 8]
                v.reduce_sum(sm_, lg, axis=AX.X)
                s.sV.inc(v.reduce_max(m1, lg, axis=AX.X))
                v.wait_ge(s.sV.h, s.sV.n)
                v.reciprocal(iv, sm_)
                s.sV.inc(v.tensor_scalar(cb, lg, m1, None, OP.is_ge))
                v.wait_ge(s.sV.h, s.sV.n)
                s.sV.inc(v.scalar_tensor_tensor(cb, cb, -1e30, lg, OP.mult, OP.add))
                v.wait_ge(s.sV.h, s.sV.n)
                s.sV.inc(v.reduce_max(m2, cb, axis=AX.X))
                v.wait_ge(s.sV.h, s.sV.n)
                s.sV.inc(v.tensor_scalar(cb, lg, m2, None, OP.is_ge))
                v.wait_ge(s.sV.h, s.sV.n)
                s.sV.inc(v.tensor_tensor(cb, cb, lg, OP.mult))
                v.wait_ge(s.sV.h, s.sV.n)
                s.sV.inc(v.tensor_scalar(cb, cb, iv, None, OP.mult))
            self.V(sm)
        tok_comb = s.sV.n

        self._ldw2(0)
        for e in range(NEXP):
            self._gphase(e, fm_tok, tok_comb)
            if e > 0:
                self._w2phase(e - 1, tok_comb)
        self._w2phase(NEXP - 1, tok_comb)
        s.wv_used = [s.w2_used, s.w2_used]
        s.projbank_free = [s._gp_tok, s._gp_tok]
        s.projbank_freeS = [s._silu_tok, s._silu_tok]
        s.ffn_pass += 1

        tok_y = s.sV.n

        def st(sy, sidx=sidx, tok_y=tok_y):
            sy.wait_ge(s.sV.h, tok_y)
            s.outS.pre(sy)
            s.outS.dma(sy.dma_start(
                s.out[sidx * SUB:(sidx + 1) * SUB, :].rearrange("(a p) d -> p a d", p=128),
                s.yac[:, :].rearrange("p (a d) -> p a d", d=D)))
        self.SY(st)
        s.yac_free = s.outS.n

    def _w2sl(self, ic):
        s = self
        if ic < 4:
            return s.ob[:, ic * 2048:(ic + 1) * 2048]
        return s.wC[:, (ic - 4) * 2048:(ic - 3) * 2048]

    def _ldw2(self, e):
        s = self

        first = s.ffn_pass == 0

        stream = s.w2S if first else s.w2Sh

        def ldw2(gp, e=e, first=first, stream=stream):
            gp.wait_ge(s.sT.h, max(s.w2_used, s.wo_used[0], s.wo_used[1],
                                   s.wv_used[0], s.wv_used[1]))
            if first:
                gp.wait_ge(s.wbS.h, s._wb2_tok)
            else:
                gp.wait_ge(s.w2S.h, s.w2S.n)
            stream.pre(gp)
            for ic in range(8):
                src2 = s.w2_t[e, ic, :, :] if first else s.w2c[e, ic, :, :]
                stream.dma(gp.dma_start(self._w2sl(ic), src2))
        self.G(ldw2) if first else self.SY(ldw2)
        s._w2_ldtok[e] = (stream, stream.n)
        if first:
            tok_ld = s.w2S.n

            def wb2(sy, e=e, tok_ld=tok_ld):
                sy.wait_ge(s.w2S.h, tok_ld)
                s.wbS.pre(sy)
                for ic in range(8):
                    s.wbS.dma(sy.dma_start(s.w2c[e, ic, :, :], self._w2sl(ic)))
                s._wb2_tok = s.wbS.n
            self.SY(wb2)

    def _gphase(self, e, fm_tok, tok_comb):
        s = self
        self.tag = "expert_g"
        gbase = (e % 2) * 4096
        for ic in range(8):
            wbuf = ic % 2

            first = s.ffn_pass == 0

            stream = s.w13S[wbuf] if first else s.w13Sh[wbuf]

            def ld13(gp, e=e, ic=ic, wbuf=wbuf, first=first, stream=stream):
                gp.wait_ge(s.sT.h, max(s.w13_used[wbuf], s.wk_used[wbuf], s.wq_used[wbuf]))
                if first:
                    gp.wait_ge(s.wbS.h, s._wb_tok[wbuf])
                else:
                    gp.wait_ge(s.w13S[wbuf].h, s.w13S[wbuf].n)
                stream.pre(gp)
                src1 = s.w1_t[e, ic, :, :] if first else s.w1c[e, ic, :, :]
                src3 = s.w3_t[e, ic, :, :] if first else s.w3c[e, ic, :, :]
                stream.dma(gp.dma_start(s.wA[:, wbuf * D:(wbuf + 1) * D], src1))
                stream.dma(gp.dma_start(s.wB[:, wbuf * D:(wbuf + 1) * D], src3))
            self.G(ld13) if first else self.SY(ld13)
            tok_w = (stream, stream.n)
            if first:
                def wb13(sy, e=e, ic=ic, wbuf=wbuf, tok_w=tok_w):
                    sy.wait_ge(tok_w[0].h, tok_w[1])
                    s.wbS.pre(sy)
                    s.wbS.dma(sy.dma_start(s.w1c[e, ic, :, :], s.wA[:, wbuf * D:(wbuf + 1) * D]))
                    s.wbS.dma(sy.dma_start(s.w3c[e, ic, :, :], s.wB[:, wbuf * D:(wbuf + 1) * D]))
                    s._wb_tok[wbuf] = s.wbS.n
                self.SY(wb13)

            def hmm(t, ic=ic, wbuf=wbuf, tok_w=tok_w, fm_tok=fm_tok,
                    silu_tok=getattr(s, "_silu_tok", 0), gp_tok=getattr(s, "_gp_tok", 0),
                    comb_tok=tok_comb):
                t.wait_ge(tok_w[0].h, tok_w[1])
                t.wait_ge(s.sS.h, max(fm_tok, silu_tok))
                t.wait_ge(s.sV.h, comb_tok)
                for k in range(16):
                    ins = t.matmul(s.ps[:, 0, :],
                                   s.wA[:, wbuf * D + k * 128:wbuf * D + (k + 1) * 128],
                                   s.fm0[:, k * SUB:(k + 1) * SUB], start=(k == 0), stop=(k == 15))
                s.sT.inc(ins)
                t.wait_ge(s.sV.h, gp_tok)
                for k in range(16):
                    ins = t.matmul(s.ps[:, 1, :],
                                   s.wB[:, wbuf * D + k * 128:wbuf * D + (k + 1) * 128],
                                   s.fm0[:, k * SUB:(k + 1) * SUB], start=(k == 0), stop=(k == 15))
                s.sT.inc(ins)
            self.T(hmm)
            s.w13_used[wbuf] = s.sT.n
            tok_h = s.sT.n

            def sl(sc, tok_h=tok_h - 1):
                sc.wait_ge(s.sT.h, tok_h)
                sc.wait_ge(s.sS.h, s.sS.n)
                sc.wait_ge(s.sV.h, s.sV.n)
                s.sS.inc(sc.activation(s.sqb[:, :], s.ps[:, 0, :], AF.Silu))
            self.S(sl)
            s._silu_tok = s.sS.n
            tok_sl = s.sS.n

            def gp_(v, e=e, ic=ic, gbase=gbase, tok_sl=tok_sl, tok_h=tok_h):
                v.wait_ge(s.sS.h, tok_sl)
                v.wait_ge(s.sT.h, max(tok_h, s.w2_used))
                v.wait_ge(s.sV.h, s.sV.n)
                s.sV.inc(v.tensor_tensor(s.vcb[:, gbase + ic * SUB:gbase + (ic + 1) * SUB],
                                         s.sqb[:, :], s.ps[:, 1, :], OP.mult))
            self.V(gp_)
            s._gp_tok = s.sV.n
        s._g_done_prev, s._g_done = s._g_done, s.sV.n

    def _w2phase(self, e, tok_comb):
        s = self
        self.tag = "expert_w2"
        gbase = (e % 2) * 4096
        g_done = s._g_done if e == NEXP - 1 else s._g_done_prev
        for tb in range(NTB):
            def w2mm(t, e=e, tb=tb, gbase=gbase, g_done=g_done):
                t.wait_ge(s._w2_ldtok[e][0].h, s._w2_ldtok[e][1])
                t.wait_ge(s.sV.h, g_done)
                for ic in range(8):
                    src = self._w2sl(ic)
                    for g in range(4):
                        if ic == 0:
                            t.wait_ge(s.sS.h, s._ycp_tok[g])
                        ins = t.matmul(
                            s.ps[:, 2 + g, :],
                            s.vcb[:, gbase + ic * SUB + tb * 128:gbase + ic * SUB + (tb + 1) * 128],
                            src[:, g * 512:(g + 1) * 512],
                            start=(ic == 0), stop=(ic == 7))
                s.sT.inc(ins)
            self.T(w2mm)
            s.w2_used = s.sT.n
            tok_mm = s.sT.n

            for half in range(2):
                def ycp(sc, tb=tb, half=half, tok_mm=tok_mm):
                    sc.wait_ge(s.sT.h, tok_mm)
                    v_tok = s._yst_tok[half]
                    sc.wait_ge(s.sV.h, v_tok)
                    for gg in range(2):
                        g = half * 2 + gg
                        sc.wait_ge(s.sS.h, s.sS.n)
                        ins = sc.copy(s.kdec[:, gg * 512:(gg + 1) * 512] if half == 0 else
                                      s.krem[:, gg * 512:(gg + 1) * 512], s.ps[:, 2 + g, :])
                        s.sS.inc(ins)
                        s._ycp_tok[g] = s.sS.n
                self.S(ycp)

                def yst(v, e=e, tb=tb, half=half):
                    v.wait_ge(s.sS.h, s._ycp_tok[half * 2 + 1])
                    v.wait_ge(s.sV.h, max(tok_comb, s.sV.n))
                    if e == 0 and tb == 0 and half == 0:
                        v.wait_ge(s.outS.h, s.yac_free)
                    stg = s.kdec[:, 0:1024] if half == 0 else s.krem[:, 0:1024]
                    ya = s.yac[:, tb * D + half * 1024:tb * D + (half + 1) * 1024]
                    if e == 0:
                        ins = v.scalar_tensor_tensor(ya, stg,
                                                     s.comb[:, tb * 8 + e:tb * 8 + e + 1],
                                                     s.y1g[:, tb * D + half * 1024:tb * D + (half + 1) * 1024],
                                                     OP.mult, OP.add)
                    elif e < 8:
                        ins = v.scalar_tensor_tensor(ya, stg,
                                                     s.comb[:, tb * 8 + e:tb * 8 + e + 1],
                                                     ya, OP.mult, OP.add)
                    else:
                        ins = v.tensor_tensor(ya, stg, ya, OP.add)
                    s.sV.inc(ins)
                    s._yst_tok[half] = s.sV.n
                self.V(yst)
        if e + 1 < NEXP:
            self._ldw2(e + 1)


def _tile_weights(inp):
    Wq = np.asarray(inp["L0_attn_Wq"])
    Wk = np.asarray(inp["L0_attn_Wk"])
    Wv = np.asarray(inp["L0_attn_Wv"])
    Wo = np.asarray(inp["L0_attn_Wo"])
    nw = np.asarray(inp["L0_attn_norm_w"])
    gw = np.asarray(inp["L0_ffn_gate_w"])
    w1 = np.asarray(inp["L0_ffn_w1"])
    w2 = np.asarray(inp["L0_ffn_w2"])
    w3 = np.asarray(inp["L0_ffn_w3"])
    ws1 = np.asarray(inp["L0_ffn_ws1"])
    ws2 = np.asarray(inp["L0_ffn_ws2"])
    ws3 = np.asarray(inp["L0_ffn_ws3"])

    def qk_tile(W):
        return np.ascontiguousarray(
            W.reshape(H, 128, 16, 128).transpose(0, 3, 2, 1).reshape(H, 128, D))

    wq_t = qk_tile(Wq)
    wk_t = qk_tile(Wk)
    wv_t = np.ascontiguousarray(
        Wv.reshape(8, 256, 16, 128).transpose(0, 3, 2, 1).reshape(8, 128, 16 * 256))
    wo_t = np.ascontiguousarray(
        Wo.reshape(4, 512, 2, 8, 128).transpose(0, 2, 4, 3, 1).reshape(4, 2, 128, 8 * 512))
    nw_t = np.ascontiguousarray(nw.reshape(16, 128).T)
    gw_t = np.ascontiguousarray(gw.reshape(8, 16, 128).transpose(2, 1, 0).reshape(128, 16 * 8))

    w1all = np.concatenate([w1, ws1[None]], axis=0)
    w3all = np.concatenate([w3, ws3[None]], axis=0)
    w2all = np.concatenate([w2, ws2[None]], axis=0)
    w13 = lambda w: np.ascontiguousarray(
        w.reshape(NEXP, 8, 128, 16, 128).transpose(0, 1, 4, 3, 2).reshape(NEXP, 8, 128, 16 * 128))
    w1_t = w13(w1all)
    w3_t = w13(w3all)
    w2_t = np.ascontiguousarray(w2all.transpose(0, 2, 1).reshape(NEXP, 8, 128, D))
    return dict(wq_t=wq_t, wk_t=wk_t, wv_t=wv_t, wo_t=wo_t, nw_t=nw_t, gw_t=gw_t,
                w1_t=w1_t, w3_t=w3_t, w2_t=w2_t)


def make_in_maps(inp):
    x = np.asarray(inp["L0_attn_x"])
    w = _tile_weights(inp)
    ones = np.ones((128, 1), np.float32)
    zeros = np.zeros((128, 1), np.float32)
    in_maps = []
    for core in range(8):
        b, half = core // 2, core % 2
        m = dict(w)
        m["x_own"] = np.ascontiguousarray(x[b, half * TOK:(half + 1) * TOK, :])
        m["x_pre"] = np.ascontiguousarray(x[b, 0:TOK, :])
        m["pre_sc"] = ones if half == 1 else zeros
        in_maps.append(m)
    return in_maps


def kernel(**inp):
    in_maps = make_in_maps(inp)
    nc = Builder().build()
    res = run_bass_kernel_spmd(nc, in_maps, core_ids=list(range(8)))
    out = np.empty((4, 4096, D), np.float32)
    for core in range(8):
        b, half = core // 2, core % 2
        out[b, half * TOK:(half + 1) * TOK, :] = res.results[core]["out"]
    return out


if __name__ == "__main__":
    nc = Builder().build()
    print("instructions:", sum(1 for _ in nc.all_instructions()))
